# version 10
# speedup vs baseline: 1.2145x; 1.0096x over previous
import os
import contextlib
import math
import numpy as np
import ml_dtypes
import concourse.bass as bass
import concourse.mybir as mybir
from concourse.bass_utils import run_bass_kernel_spmd

F32 = mybir.dt.float32
BF16 = mybir.dt.bfloat16
I32 = mybir.dt.int32
AF = mybir.ActivationFunctionType
ALU = mybir.AluOpType
AX = mybir.AxisListType

NCORES = 8
B_PER_CORE = 2
S = 4096
T = B_PER_CORE * S
D = 1024
KT = D // 128
NTILE = T // 128

BF = ml_dtypes.bfloat16


def ts(i, n=128):
    return slice(i * n, (i + 1) * n)



ENGINES = ("t", "v", "s", "g", "y")
_ENG_ATTR = {"t": "tensor", "v": "vector", "s": "scalar", "g": "gpsimd", "y": "sync"}


class Reg:
    __slots__ = ("name", "const", "excl", "last_w", "readers")

    def __init__(self, name, const=False, excl=False):
        self.name = name
        self.const = const
        self.excl = excl
        self.last_w = None
        self.readers = []


DMA_POOL = 12


class Prog:

    def __init__(self, nc, self_sync=True):
        self.nc = nc
        self.self_sync = self_sync
        self.stream = {e: [] for e in ENGINES}
        self.cnt = {"t": 1, "v": 1, "s": 1, "y": 0, "g": 0}
        self.dmacnt = {"y": 0, "g": 0}
        self.dval = {}
        for q in ("y", "g"):
            for i in range(DMA_POOL):
                self.dval[(q, i)] = 0
        self.regs = []

    def curval(self, key):
        return self.cnt[key] if isinstance(key, str) else self.dval[key]

    def reg(self, name, const=False, excl=False):
        r = Reg(name, const, excl)
        self.regs.append(r)
        return r

    def regs_n(self, name, n, excl=False):
        return [self.reg(f"{name}{i}", excl=excl) for i in range(n)]

    def preg(self, name):
        return self.reg(name, excl=True)

    def pregs_n(self, name, n):
        return [self.reg(f"{name}{i}", excl=True) for i in range(n)]

    def _region_waits(self, eng, reads, writes, waits, tag):
        reads = list(reads)
        writes = list(writes)
        excl = [r for r in reads if r.excl]
        if excl:
            reads = [r for r in reads if not r.excl]
            writes = writes + [r for r in excl if r not in writes]
        ws = [ev for ev in waits if ev is not None]
        for r in reads:
            if r.last_w is not None:
                ws.append(r.last_w)
        for r in writes:
            assert not r.const, f"write to const region {r.name}"
            if r.last_w is not None:
                ws.append(r.last_w)
            ws.extend(r.readers)
        ws2 = []
        for key, wv in ws:
            assert wv <= self.curval(key), f"future wait {key}@{wv} tag={tag}"
            if key == eng and (eng == "t" or
                               (not self.self_sync and eng in ("v", "s"))):
                continue
            ws2.append((key, wv))
        return reads, writes, ws2

    def _finish(self, reads, writes, ev):
        for r in reads:
            if not r.const:
                r.readers.append(ev)
        for r in writes:
            r.last_w = ev
            r.readers = []
        return ev

    def emit(self, eng, fn, reads=(), writes=(), waits=(), sig=1, tag="",
             pool=False):
        if eng in ("y", "g") and fn is not None and not pool:
            return self.emit_dma(eng, fn, reads, writes, waits, tag)
        reads, writes, ws2 = self._region_waits(eng, reads, writes, waits, tag)
        kind = "eng" if (fn is not None and sig) else None
        self.stream[eng].append((fn, ws2, kind, tag))
        if kind == "eng":
            self.cnt[eng] += 1
        ev = (eng, self.cnt[eng])
        return self._finish(reads, writes, ev)

    def emit_dma(self, q, fn, reads=(), writes=(), waits=(), tag=""):
        i = self.dmacnt[q]
        self.dmacnt[q] += 1
        key = (q, i % DMA_POOL)
        use = i // DMA_POOL + 1
        waits = list(waits)
        if use > 1:
            waits.append((key, 16 * (use - 1)))
        reads, writes, ws2 = self._region_waits(q, reads, writes, waits, tag)
        self.stream[q].append((fn, ws2, ("dma", key), tag))
        self.dval[key] = 16 * use
        ev = (key, 16 * use)
        return self._finish(reads, writes, ev)

    def ev(self, eng):
        return (eng, self.cnt[eng])

    def barrier(self, tag="bar"):
        evs = []
        for q in ("y", "g"):
            ws = [(k, v) for k, v in self.dval.items() if k[0] == q and v > 0]
            self.cnt[q] += 1
            self.stream[q].append(("MILE", ws, None, tag))
            evs.append((q, self.cnt[q]))
        for e in ("t", "v", "s"):
            evs.append((e, self.cnt[e]))
        for e in ENGINES:
            ws = [ev for ev in evs if ev[0] != e]
            self.stream[e].append((None, ws, None, tag))
        for r in self.regs:
            r.last_w = None
            r.readers = []

    def build(self, prologue=True):
        nc = self.nc
        keys = ["t", "v", "s", "y", "g"] + list(self.dval.keys())
        sems = {}
        ctxs = []
        for k in keys:
            nm = f"sem_{k}" if isinstance(k, str) else f"sem_{k[0]}d{k[1]}"
            cm = nc.semaphore(nm)
            sems[k] = cm.__enter__()
            ctxs.append(cm)
        blk_cm = nc.Block()
        blk = blk_cm.__enter__()
        for e in ENGINES:
            def make_body(eng):
                def body(engobj):
                    if eng in ("y", "g"):
                        engobj.sem_clear(sems[eng])
                        for i in range(DMA_POOL):
                            engobj.sem_clear(sems[(eng, i)])
                    else:
                        engobj.sem_clear(sems[eng])
                        engobj.sem_inc(sems[eng], 1)
                    if prologue:
                        for _ in range(3):
                            engobj.nop(cycle_cnt=50000)
                    waited = {k: 0 for k in keys}
                    for fn, ws, kind, tag in self.stream[eng]:
                        for (k, wv) in ws:
                            if wv > waited[k]:
                                engobj.wait_ge(sems[k], wv)
                                waited[k] = wv
                        if fn is None:
                            continue
                        if fn == "MILE":
                            engobj.sem_inc(sems[eng], 1)
                            continue
                        inst = fn(engobj)
                        if kind == "eng":
                            assert inst is not None, f"op None tag={tag}"
                            inst.then_inc(sems[eng], 1)
                        elif kind is not None and kind[0] == "dma":
                            assert inst is not None, f"dma None tag={tag}"
                            inst.then_inc(sems[kind[1]], 16)
                return body
            getattr(blk, _ENG_ATTR[e])(make_body(e))
        blk_cm.__exit__(None, None, None)
        for cm in ctxs:
            cm.__exit__(None, None, None)

    def stats(self):
        return {e: len(self.stream[e]) for e in ENGINES}



def build_consts():
    c = {}
    c["c_ident_bf"] = np.eye(128, dtype=BF)
    H0 = 8
    gam = 1.0 - np.exp2(-5.0 - np.arange(H0, dtype=np.float64))
    i_idx = np.arange(128)
    dec = np.zeros((H0, 128, 128), np.float64)
    qg = np.zeros((H0, 128, 128), np.float64)
    vg = np.zeros((H0, 128), np.float64)
    for h in range(H0):
        e = i_idx[None, :] - i_idx[:, None]
        dec[h] = np.where(e >= 0, gam[h] ** e, 0.0) / math.sqrt(128.0)
        qg[h] = np.broadcast_to(gam[h] ** (i_idx + 1.0) / math.sqrt(128.0),
                                (128, 128))
        vg[h] = gam[h] ** (127.0 - i_idx)
    c["c_l0_dec"] = np.ascontiguousarray(dec.transpose(1, 0, 2)).astype(np.float32)
    c["c_l0_qg"] = np.ascontiguousarray(qg.transpose(1, 0, 2)).astype(np.float32)
    c["c_l0_vg"] = np.ascontiguousarray(vg.T).astype(np.float32)
    c["c_iota16"] = np.broadcast_to(np.arange(16, dtype=np.float32), (128, 16)).copy()
    c["c_iota8"] = np.broadcast_to(np.arange(8, dtype=np.float32), (128, 8)).copy()
    c["c_ones1_bf"] = np.ones((1, 128), dtype=BF)
    c["c_ones128_bf"] = np.ones((128, 1), dtype=BF)
    c["c_neg16"] = np.full((128, 16), -3.0e38, np.float32)
    c.update(l2_rotary_consts())
    i32 = np.arange(32)
    i64 = np.arange(64)
    c["c_l3_m1"] = (i64[None, :] > i32[:, None]).astype(np.float32)
    m2 = np.zeros((64, 32), np.float32)
    m2[32:64, :] = (np.arange(32)[None, :] > np.arange(32)[:, None]).astype(np.float32)
    c["c_l3_m2"] = m2
    m12 = np.zeros((64, 64), np.float32)
    m12[0:32, :] = (i64[None, :] > i32[:, None]).astype(np.float32)
    m12[32:64, 32:64] = (np.arange(32)[None, :] > np.arange(32)[:, None]
                         ).astype(np.float32)
    c["c_l3_m12"] = m12
    return c


L0_GAMMA_C = ((1.0 - np.exp2(-5.0 - np.arange(8, dtype=np.float64))) ** 128.0).tolist()




_NAMECNT = {}


def _uniq(name):
    k = _NAMECNT.get(name, 0)
    _NAMECNT[name] = k + 1
    return name if k == 0 else f"{name}__{k}"


def sb(st, nc, name, shape, dt):
    return st.enter_context(nc.sbuf_tensor(_uniq(name), shape, dt))


def pst(st, nc, name, shape, dt=F32):
    return st.enter_context(nc.psum_tensor(_uniq(name), shape, dt))


def eps_tile(p, st, name, val):
    t = st.enter_context(p.nc.sbuf_tensor(_uniq(name), [128, 1], F32))
    r = p.reg(name)
    p.emit("v", lambda v: v.memset(t[:, :], float(val)), writes=[r],
           tag=f"eps_{name}")
    r.const = True
    return t, r

def wload(p, st, name, hbm_ap, K, N, dt=BF16):
    nc = p.nc
    kt = K // 128
    sb = st.enter_context(nc.sbuf_tensor(_uniq(f"w_{name}"), [128, kt, N], dt))
    r = p.reg(f"w_{name}")
    src = hbm_ap.rearrange("(kt p) n -> p kt n", p=128)
    p.emit("g", lambda g: g.dma_start(out=sb[:, :, :], in_=src), writes=[r],
           sig=16, tag=f"wl_{name}")
    r.const = True
    return sb, r


def cload(p, st, name, hbm_ap, shape, dt=F32):
    nc = p.nc
    sb = st.enter_context(nc.sbuf_tensor(_uniq(f"c_{name}"), list(shape), dt))
    r = p.reg(f"c_{name}")
    p.emit("g", lambda g: g.dma_start(out=sb[...], in_=hbm_ap), writes=[r],
           sig=16, tag=f"cl_{name}")
    r.const = True
    return sb, r


def slot_load(p, dst_ap, src_ap, rdst, eng="g", tag="sl"):
    return p.emit(eng, lambda q: q.dma_start(out=dst_ap, in_=src_ap),
                  writes=[rdst], sig=16, tag=tag)



def phase_ln(p, st, x_hbm, xn_hbm, xnT_hbm, inp, want_xn=True):
    nc = p.nc
    idb, ridb = cload(p, st, "lnid", inp["c_ident_bf"], [128, 128], BF16)
    epsl, repsl = eps_tile(p, st, "ln_eps", 1e-5)
    NSL = 4
    with contextlib.ExitStack() as es:
        xt = sb(es, nc, "ln_x", [128, NSL, D], F32)
        sq = sb(es, nc, "ln_sq", [128, D], F32)
        stt = sb(es, nc, "ln_st", [128, NSL, 8], F32)
        ot = sb(es, nc, "ln_o", [128, NSL, D], F32)
        bt = sb(es, nc, "ln_b", [128, 4, D], BF16)
        tt = sb(es, nc, "ln_t", [128, 4, D], BF16)
        ps = pst(es, nc, "ln_ps", [128, 4, D], BF16)
        rx = p.regs_n("ln_x", NSL)
        rst = p.regs_n("ln_st", NSL)
        ro = p.regs_n("ln_o", NSL)
        rb = p.regs_n("ln_b", 4)
        rt = p.regs_n("ln_t", 4)
        rp = p.pregs_n("ln_ps", 4)
        rsq = p.reg("ln_sq")
        for i in range(NTILE):
            a, b2 = i % NSL, i % 4
            p.emit("y", lambda y, i=i, a=a: y.dma_start(
                out=xt[:, a, :], in_=x_hbm[ts(i), :]), writes=[rx[a]], sig=16,
                tag="ln_ld")
            p.emit("s", lambda s, a=a: s.activation(
                sq[:, :], xt[:, a, :], AF.Square, accum_out=stt[:, a, 1:2]),
                reads=[rx[a]], writes=[rst[a], rsq], tag="ln_sq")
            p.emit("s", lambda s, a=a: s.activation(
                sq[:, :], xt[:, a, :], AF.Identity, accum_out=stt[:, a, 0:1]),
                reads=[rx[a]], writes=[rst[a], rsq], tag="ln_sum")
            p.emit("s", lambda s, a=a: s.activation(
                stt[:, a, 2:3], stt[:, a, 0:1], AF.Identity, scale=-1.0 / D),
                reads=[rst[a]], writes=[rst[a]], tag="ln_nm")
            p.emit("s", lambda s, a=a: s.activation(
                stt[:, a, 3:4], stt[:, a, 2:3], AF.Square),
                reads=[rst[a]], writes=[rst[a]], tag="ln_m2")
            p.emit("s", lambda s, a=a: s.activation(
                stt[:, a, 4:5], stt[:, a, 3:4], AF.Identity, scale=-1.0,
                bias=epsl[:, :]),
                reads=[rst[a], repsl], writes=[rst[a]], tag="ln_em2")
            p.emit("s", lambda s, a=a: s.activation(
                stt[:, a, 5:6], stt[:, a, 1:2], AF.Sqrt, scale=1.0 / D,
                bias=stt[:, a, 4:5]),
                reads=[rst[a]], writes=[rst[a]], tag="ln_std")
            p.emit("v", lambda v, a=a: v.reciprocal(
                stt[:, a, 6:7], stt[:, a, 5:6]),
                reads=[rst[a]], writes=[rst[a]], tag="ln_rstd")
            p.emit("v", lambda v, a=a: v.tensor_scalar(
                ot[:, a, :], xt[:, a, :], stt[:, a, 2:3], stt[:, a, 6:7],
                ALU.add, ALU.mult),
                reads=[rx[a], rst[a]], writes=[ro[a]], tag="ln_xn")
            if want_xn:
                p.emit("g", lambda g, i=i, a=a: g.dma_start(
                    out=xn_hbm[ts(i), :], in_=ot[:, a, :]),
                    reads=[ro[a]], sig=16, tag="ln_stxn")
            p.emit("v", lambda v, a=a, b2=b2: v.tensor_copy(
                bt[:, b2, :], ot[:, a, :]), reads=[ro[a]], writes=[rb[b2]],
                tag="ln_bf")
            def f_tp(t, b2=b2):
                last = None
                for j in range(KT):
                    last = t.transpose(ps[:, b2, ts(j)], bt[:, b2, ts(j)],
                                       idb[:, :])
                return last
            p.emit("t", f_tp, reads=[rb[b2], ridb], writes=[rp[b2]], tag="ln_tp")
            p.emit("v", lambda v, b2=b2: v.tensor_copy(
                tt[:, b2, :], ps[:, b2, :]), reads=[rp[b2]], writes=[rt[b2]],
                tag="ln_tpev")
            p.emit("g", lambda g, i=i, b2=b2: g.dma_start(
                out=xnT_hbm.rearrange("(kt r) t -> r kt t", r=128)[:, :, ts(i)],
                in_=tt[:, b2, :].rearrange("p (kt c) -> p kt c", kt=KT)),
                reads=[rt[b2]], sig=16, tag="ln_stxnT")
    p.barrier(tag="ln_end")



def phase_l0_attn(p, xnT_hbm, attn_hbm, inp, dbg=None):
    nc = p.nc
    H = 8
    C = 128
    NCH = S // C
    with contextlib.ExitStack() as st:
        wq, rwq = wload(p, st, "l0wq", inp["L0_attn_Wq"], D, D)
        wk, rwk = wload(p, st, "l0wk", inp["L0_attn_Wk"], D, D)
        wv, rwv = wload(p, st, "l0wv", inp["L0_attn_Wv"], D, D)
        wg, rwgw = wload(p, st, "l0wg", inp["L0_attn_W_G"], D, D)
        wo, rwo = wload(p, st, "l0wo", inp["L0_attn_W_O"], D, D)
        dec, rdec = cload(p, st, "l0dec", inp["c_l0_dec"], [128, H, 128])
        qg, rqg = cload(p, st, "l0qg", inp["c_l0_qg"], [128, H, 128])
        vg, rvg = cload(p, st, "l0vg", inp["c_l0_vg"], [128, H])
        idb, ridb = cload(p, st, "l0id", inp["c_ident_bf"], [128, 128], BF16)
        epsa, repsa = eps_tile(p, st, "a0_eps", 1e-6)
        with contextlib.ExitStack() as es:
            xnT = sb(es, nc, "a0_xnT", [128, 2, KT, C], BF16)
            qT = sb(es, nc, "a0_qT", [128, H, C], BF16)
            kT = sb(es, nc, "a0_kT", [128, H, C], BF16)
            vB = sb(es, nc, "a0_vB", [128, H, C], BF16)
            vS = sb(es, nc, "a0_vs", [128, C], BF16)
            kB = sb(es, nc, "a0_kB", [128, C], BF16)
            At = sb(es, nc, "a0_At", [128, C], BF16)
            qs = sb(es, nc, "a0_qs", [128, C], BF16)
            Zf = sb(es, nc, "a0_Zf", [128, H, 128], F32)
            Zb = sb(es, nc, "a0_Zb", [128, H, 128], BF16)
            ysb = sb(es, nc, "a0_y", [128, H, 128], F32)
            sqs = sb(es, nc, "a0_sq", [128, H, 128], F32)
            gst = sb(es, nc, "a0_st", [128, 5, 8], F32)
            G = sb(es, nc, "a0_G", [128, D], BF16)
            gt = sb(es, nc, "a0_gt", [128, D], BF16)
            gTT = sb(es, nc, "a0_gT", [128, D], BF16)
            osb = sb(es, nc, "a0_o", [128, 2, D], F32)
            p_qk = pst(es, nc, "a0_p_qk", [128, 512], F32)
            p_st = pst(es, nc, "a0_p_st", [128, 128], F32)
            p_y = pst(es, nc, "a0_p_y", [128, 128], F32)
            p_z = pst(es, nc, "a0_p_z", [128, 128], F32)
            p_tp = pst(es, nc, "a0_p_tp", [128, 1024], BF16)
            p_o = pst(es, nc, "a0_p_o", [128, 1024], F32)
            rxn = p.regs_n("a0_xnT", 2)
            rq, rk, rv = p.reg("a0_qT"), p.reg("a0_kT"), p.reg("a0_vB")
            rvs, rkb = p.reg("a0_vs"), p.reg("a0_kB")
            rat, rqs = p.reg("a0_At"), p.reg("a0_qs")
            rz, rzb = p.reg("a0_Zf"), p.reg("a0_Zb")
            ry, rsq, rgst = p.reg("a0_y"), p.reg("a0_sq"), p.reg("a0_st")
            rg, rgt, rgT = p.reg("a0_G"), p.reg("a0_gt"), p.reg("a0_gT")
            ro = p.regs_n("a0_o", 2)
            rp_qk, rp_st = p.preg("a0_p_qk"), p.preg("a0_p_st")
            rp_y, rp_z = p.preg("a0_p_y"), p.preg("a0_p_z")
            rp_tp, rp_o = p.preg("a0_p_tp"), p.preg("a0_p_o")
            xnTr = xnT_hbm.rearrange("(kt r) t -> r kt t", r=128)
            for bi in range(B_PER_CORE):
                p.emit("v", lambda v: v.memset(Zf[:, :, :], 0.0), writes=[rz],
                       tag="z0")
                p.emit("v", lambda v: v.memset(Zb[:, :, :], 0.0), writes=[rzb],
                       tag="z0b")
                for ci in range(NCH):
                    sl = ci % 2
                    ttile = (bi * S + ci * C) // 128
                    p.emit("y", lambda y, sl=sl, ttile=ttile: y.dma_start(
                        out=xnT[:, sl, :, :], in_=xnTr[:, :, ts(ttile)]),
                        writes=[rxn[sl]], sig=16, tag="a0_xc")
                    for which, wmat, rdst, dst in (
                            ("q", wq, rq, qT), ("k", wk, rk, kT)):
                        for half in range(2):
                            def f_qk(t, wmat=wmat, half=half, sl=sl):
                                last = None
                                for n2 in range(4):
                                    n = half * 4 + n2
                                    for k in range(KT):
                                        last = t.matmul(
                                            p_qk[:, ts(n2)], wmat[:, k, ts(n)],
                                            xnT[:, sl, k, :],
                                            start=(k == 0), stop=(k == KT - 1))
                                return last
                            p.emit("t", f_qk, reads=[rxn[sl], rwq, rwk], writes=[rp_qk],
                                   tag=f"a0_{which}mm")
                            p.emit("v", lambda v, half=half, dst=dst:
                                   v.tensor_copy(
                                       dst[:, ts(half, 4), :].rearrange(
                                           "p h c -> p (h c)"), p_qk[:, :]),
                                   reads=[rp_qk],
                                   writes=[rdst], tag=f"a0_{which}ev")
                    for half in range(2):
                        def f_v(t, half=half, sl=sl):
                            last = None
                            for k in range(KT):
                                last = t.matmul(p_o[:, ts(half, 512)],
                                                xnT[:, sl, k, :],
                                                wv[:, k, ts(half, 512)],
                                                start=(k == 0),
                                                stop=(k == KT - 1))
                            return last
                        p.emit("t", f_v, reads=[rxn[sl], rwv], writes=[rp_o],
                               tag="a0_vmm")
                        p.emit("v", lambda v, half=half: v.tensor_copy(
                            vB[:, ts(half, 4), :].rearrange("p h c -> p (h c)"),
                            p_o[:, ts(half, 512)]), reads=[rp_o], writes=[rv],
                            tag="a0_vev")
                    for h in range(H):
                        p.emit("t", lambda t, h=h: t.matmul(
                            p_st[:, :], kT[:, h, :], qT[:, h, :],
                            start=True, stop=True),
                            reads=[rq, rk], writes=[rp_st], tag="a0_st")
                        p.emit("v", lambda v, h=h: v.tensor_mul(
                            At[:, :], p_st[:, :], dec[:, h, :]),
                            reads=[rp_st, rdec], writes=[rat], tag="a0_A")
                        p.emit("v", lambda v, h=h: v.tensor_mul(
                            qs[:, :], qT[:, h, :], qg[:, h, :]),
                            reads=[rq, rqg], writes=[rqs], tag="a0_qs")
                        def f_y(t, h=h):
                            t.matmul(p_y[:, :], At[:, :], vB[:, h, :],
                                     start=True, stop=False)
                            return t.matmul(p_y[:, :], qs[:, :], Zb[:, h, :],
                                            start=False, stop=True)
                        p.emit("t", f_y, reads=[rat, rqs, rv, rzb],
                               writes=[rp_y], tag="a0_ymm")
                        p.emit("s", lambda s, h=h: s.copy(
                            ysb[:, h, :], p_y[:, :]), reads=[rp_y],
                            writes=[ry], tag="a0_yev")
                        p.emit("t", lambda t, h=h: t.transpose(
                            p_tp[:, 0:128], kT[:, h, :], idb[:, :]),
                            reads=[rk, ridb], writes=[rp_tp], tag="a0_ktp")
                        p.emit("v", lambda v: v.tensor_copy(
                            kB[:, :], p_tp[:, 0:128]), reads=[rp_tp],
                            writes=[rkb], tag="a0_kbe")
                        p.emit("v", lambda v, h=h: v.tensor_scalar_mul(
                            vS[:, :], vB[:, h, :], vg[:, h:h + 1]),
                            reads=[rv, rvg], writes=[rvs], tag="a0_vs")
                        p.emit("t", lambda t: t.matmul(
                            p_z[:, :], kB[:, :], vS[:, :], start=True,
                            stop=True),
                            reads=[rkb, rvs], writes=[rp_z], tag="a0_zmm")
                        p.emit("v", lambda v, h=h: v.scalar_tensor_tensor(
                            Zf[:, h, :], Zf[:, h, :], float(L0_GAMMA_C[h]),
                            p_z[:, :], ALU.mult, ALU.add),
                            reads=[rz, rp_z], writes=[rz], tag="a0_zup")
                        p.emit("v", lambda v, h=h: v.tensor_copy(
                            Zb[:, h, :], Zf[:, h, :]), reads=[rz],
                            writes=[rzb], tag="a0_zb")
                    p.emit("v", lambda v: v.reduce_sum(
                        gst[:, 0, :], ysb[:, :, :], axis=AX.X),
                        reads=[ry], writes=[rgst], tag="a0_gsum")
                    p.emit("v", lambda v: v.tensor_mul(
                        sqs[:, :, :], ysb[:, :, :], ysb[:, :, :]),
                        reads=[ry], writes=[rsq], tag="a0_gsq")
                    p.emit("v", lambda v: v.reduce_sum(
                        gst[:, 1, :], sqs[:, :, :], axis=AX.X),
                        reads=[rsq], writes=[rgst], tag="a0_gsq2")
                    p.emit("v", lambda v: v.tensor_scalar_mul(
                        gst[:, 2, :], gst[:, 0, :], -1.0 / 128.0),
                        reads=[rgst], writes=[rgst], tag="a0_gnm")
                    p.emit("v", lambda v: v.tensor_mul(
                        gst[:, 3, :], gst[:, 2, :], gst[:, 2, :]),
                        reads=[rgst], writes=[rgst], tag="a0_gm2")
                    p.emit("v", lambda v: v.scalar_tensor_tensor(
                        gst[:, 4, :], gst[:, 1, :], 1.0 / 128.0, gst[:, 3, :],
                        ALU.mult, ALU.subtract),
                        reads=[rgst], writes=[rgst], tag="a0_gvar")
                    p.emit("s", lambda s: s.activation(
                        gst[:, 1, :], gst[:, 4, :], AF.Sqrt, bias=epsa[:, :]),
                        reads=[rgst, repsa], writes=[rgst], tag="a0_gstd")
                    p.emit("v", lambda v: v.reciprocal(
                        gst[:, 0, :], gst[:, 1, :]),
                        reads=[rgst], writes=[rgst], tag="a0_grs")
                    if dbg is not None:
                        p.emit("g", lambda g, ttile=ttile: g.dma_start(
                            out=dbg["yraw"][ts(ttile), :],
                            in_=ysb[:, :, :].rearrange("p h c -> p (h c)")),
                            reads=[ry], sig=16, tag="dbg_y")
                        p.emit("g", lambda g, ttile=ttile: g.dma_start(
                            out=dbg["gvar"][ts(ttile), :], in_=gst[:, 4, :]),
                            reads=[rgst], sig=16, tag="dbg_gv")
                    for half in range(2):
                        def f_g(t, half=half, sl=sl):
                            last = None
                            for k in range(KT):
                                last = t.matmul(p_o[:, ts(half, 512)],
                                                xnT[:, sl, k, :],
                                                wg[:, k, ts(half, 512)],
                                                start=(k == 0),
                                                stop=(k == KT - 1))
                            return last
                        p.emit("t", f_g, reads=[rxn[sl], rwgw], writes=[rp_o],
                               tag="a0_gmm")
                        p.emit("s", lambda s, half=half: s.activation(
                            G[:, ts(half, 512)], p_o[:, ts(half, 512)],
                            AF.Silu), reads=[rp_o], writes=[rg],
                            tag="a0_gsilu")
                    for h in range(H):
                        p.emit("v", lambda v, h=h: v.tensor_scalar(
                            ysb[:, h, :], ysb[:, h, :], gst[:, 2, h:h + 1],
                            gst[:, 0, h:h + 1], ALU.add, ALU.mult),
                            reads=[ry, rgst], writes=[ry], tag="a0_gn")
                    p.emit("v", lambda v: v.tensor_mul(
                        gt[:, :], ysb[:, :, :].rearrange("p h c -> p (h c)"),
                        G[:, :]), reads=[ry, rg], writes=[rgt], tag="a0_gt")
                    def f_gtp(t):
                        last = None
                        for j in range(KT):
                            last = t.transpose(p_tp[:, ts(j)], gt[:, ts(j)],
                                               idb[:, :])
                        return last
                    p.emit("t", f_gtp, reads=[rgt, ridb], writes=[rp_tp],
                           tag="a0_gtp")
                    p.emit("v", lambda v: v.tensor_copy(gTT[:, :], p_tp[:, :]),
                           reads=[rp_tp], writes=[rgT], tag="a0_gte")
                    def f_o(t):
                        last = None
                        for half in range(2):
                            for k in range(KT):
                                last = t.matmul(p_o[:, ts(half, 512)],
                                                gTT[:, ts(k)],
                                                wo[:, k, ts(half, 512)],
                                                start=(k == 0),
                                                stop=(k == KT - 1))
                        return last
                    p.emit("t", f_o, reads=[rgT, rwo], writes=[rp_o], tag="a0_omm")
                    p.emit("s", lambda s, sl=sl: s.copy(
                        osb[:, sl, :], p_o[:, :]),
                        reads=[rp_o], writes=[ro[sl]], tag="a0_oev")
                    p.emit("g", lambda g, ttile=ttile, sl=sl: g.dma_start(
                        out=attn_hbm[ts(ttile), :], in_=osb[:, sl, :]),
                        reads=[ro[sl]], sig=16, tag="a0_ost")
    p.barrier(tag="l0a_end")



def phase_l0_ffn(p, xnT_hbm, x_hbm, attn_hbm, xout_hbm, inp):
    nc = p.nc
    E = 16
    F = 512
    GRP = 2048
    NG = T // GRP
    NTG = GRP // 128
    xnTr = xnT_hbm.rearrange("(kt r) t -> r kt t", r=128)
    with contextlib.ExitStack() as st:
        wr, rwr = wload(p, st, "l0wr", inp["L0_ffn_Wr"], D, E)
        iota, riota = cload(p, st, "l0iota", inp["c_iota16"], [128, 16])
        neg, rneg = cload(p, st, "l0neg", inp["c_neg16"], [128, 16])
        idb, ridb = cload(p, st, "l0id2", inp["c_ident_bf"], [128, 128], BF16)
        with contextlib.ExitStack() as es:
            wgs = sb(es, nc, "f0_wg", [128, 4, KT, F], BF16)
            wus = sb(es, nc, "f0_wu", [128, 4, KT, F], BF16)
            wds = sb(es, nc, "f0_wd", [128, 4, 4, D], BF16)
            acc = sb(es, nc, "f0_acc", [128, NTG, D], F32)
            cw = sb(es, nc, "f0_cw", [128, NTG, E], F32)
            xnT = sb(es, nc, "f0_xnT", [128, 2, KT, 128], BF16)
            sgb = sb(es, nc, "f0_sg", [128, 2, F], BF16)
            hb = sb(es, nc, "f0_h", [128, 2, F], BF16)
            hT = sb(es, nc, "f0_hT", [128, 2, F], BF16)
            rsc = sb(es, nc, "f0_r", [128, 6, 16], F32)
            xsb = sb(es, nc, "f0_x", [128, 2, D], F32)
            atb = sb(es, nc, "f0_at", [128, 2, D], F32)
            outb = sb(es, nc, "f0_o", [128, 2, D], F32)
            p_gu = pst(es, nc, "f0_p_gu", [128, 2, 512], F32)
            p_dn = pst(es, nc, "f0_p_dn", [128, 1024], F32)
            p_tp = pst(es, nc, "f0_p_tp", [128, 2, 1024], BF16)
            p_r = pst(es, nc, "f0_p_r", [128, 16], F32)
            rwg = p.regs_n("f0_wg", 4)
            rwu = p.regs_n("f0_wu", 4)
            rwd = p.regs_n("f0_wd", 4)
            racc, rcw = p.reg("f0_acc"), p.reg("f0_cw")
            rxn = p.regs_n("f0_xnT", 2)
            rsg = p.regs_n("f0_sg", 2)
            rh = p.regs_n("f0_h", 2)
            rhT = p.regs_n("f0_hT", 2)
            rr = p.reg("f0_r")
            rx = p.regs_n("f0_x", 2)
            rat = p.regs_n("f0_at", 2)
            rob = p.regs_n("f0_o", 2)
            rp_gu = p.pregs_n("f0_p_gu", 2)
            rp_dn = p.preg("f0_p_dn")
            rp_tp = p.pregs_n("f0_p_tp", 2)
            rp_r = p.preg("f0_p_r")

            def xload(tt, sl):
                p.emit("y", lambda y: y.dma_start(
                    out=xnT[:, sl, :, :], in_=xnTr[:, :, ts(tt)]),
                    writes=[rxn[sl]], sig=16, tag="f0_xld")

            def halfpath(sl2, rd_gate, rd_up, gate_rhs, up_rhs, rxn_sl, sl,
                         cw_ap=None, rcw_l=()):
                p.emit("t", lambda t: [
                    t.matmul(p_gu[:, 0, :], xnT[:, sl, k, :], gate_rhs(k),
                             start=(k == 0), stop=(k == KT - 1))
                    for k in range(KT)][-1],
                    reads=[rxn_sl] + rd_gate, writes=[rp_gu[0]], tag="f0_gm")
                p.emit("t", lambda t: [
                    t.matmul(p_gu[:, 1, :], xnT[:, sl, k, :], up_rhs(k),
                             start=(k == 0), stop=(k == KT - 1))
                    for k in range(KT)][-1],
                    reads=[rxn_sl] + rd_up, writes=[rp_gu[1]], tag="f0_um")
                p.emit("s", lambda s: s.activation(
                    sgb[:, sl2, :], p_gu[:, 0, :], AF.Silu),
                    reads=[rp_gu[0]], writes=[rsg[sl2]], tag="f0_silu")
                if cw_ap is None:
                    p.emit("v", lambda v: v.tensor_mul(
                        hb[:, sl2, :], p_gu[:, 1, :], sgb[:, sl2, :]),
                        reads=[rp_gu[1], rsg[sl2]], writes=[rh[sl2]],
                        tag="f0_h")
                else:
                    p.emit("v", lambda v: v.scalar_tensor_tensor(
                        hb[:, sl2, :], p_gu[:, 1, :], cw_ap, sgb[:, sl2, :],
                        ALU.mult, ALU.mult),
                        reads=[rp_gu[1], rsg[sl2]] + list(rcw_l),
                        writes=[rh[sl2]], tag="f0_hc")
                def f_tp(t):
                    last = None
                    for j in range(4):
                        last = t.transpose(p_tp[:, sl2, ts(j)],
                                           hb[:, sl2, ts(j)], idb[:, :])
                    return last
                p.emit("t", f_tp, reads=[rh[sl2], ridb], writes=[rp_tp[sl2]],
                       tag="f0_tp")
                p.emit("v", lambda v: v.tensor_copy(
                    hT[:, sl2, :], p_tp[:, sl2, 0:512]),
                    reads=[rp_tp[sl2]], writes=[rhT[sl2]], tag="f0_tpe")

            for gi in range(NG):
                for half in range(2):
                    slot_load(p, wgs[:, half, :, :],
                              inp["L0_ffn_Sg"][:, ts(half, 512)].rearrange(
                                  "(kt p) n -> p kt n", p=128), rwg[half],
                              tag="f0_lsg")
                    slot_load(p, wus[:, half, :, :],
                              inp["L0_ffn_Su"][:, ts(half, 512)].rearrange(
                                  "(kt p) n -> p kt n", p=128), rwu[half],
                              tag="f0_lsu")
                    slot_load(p, wds[:, half, :, :],
                              inp["L0_ffn_Sd"][ts(half, 512), :].rearrange(
                                  "(kt p) n -> p kt n", p=128), rwd[half],
                              tag="f0_lsd")
                for ti in range(NTG):
                    sl = ti % 2
                    tt = gi * NTG + ti
                    xload(tt, sl)
                    p.emit("t", lambda t, sl=sl: [
                        t.matmul(p_r[:, :], xnT[:, sl, k, :], wr[:, k, :],
                                 start=(k == 0), stop=(k == KT - 1))
                        for k in range(KT)][-1],
                        reads=[rxn[sl], rwr], writes=[rp_r], tag="f0_rmm")
                    p.emit("v", lambda v: v.reduce_max(
                        rsc[:, 0, 0:1], p_r[:, :], axis=AX.X, negate=True),
                        reads=[rp_r], writes=[rr], tag="f0_rmax")
                    p.emit("s", lambda s: s.activation(
                        rsc[:, 1, :], p_r[:, :], AF.Exp, bias=rsc[:, 0, 0:1],
                        accum_out=rsc[:, 0, 1:2]),
                        reads=[rp_r, rr], writes=[rr], tag="f0_rexp")
                    p.emit("v", lambda v: v.reciprocal(
                        rsc[:, 0, 2:3], rsc[:, 0, 1:2]),
                        reads=[rr], writes=[rr], tag="f0_rrec")
                    p.emit("v", lambda v: v.tensor_scalar_mul(
                        rsc[:, 2, :], rsc[:, 1, :], rsc[:, 0, 2:3]),
                        reads=[rr], writes=[rr], tag="f0_rpr")
                    p.emit("v", lambda v: v.reduce_max(
                        rsc[:, 0, 3:4], rsc[:, 2, :], axis=AX.X),
                        reads=[rr], writes=[rr], tag="f0_rm1")
                    p.emit("v", lambda v: v.tensor_tensor(
                        rsc[:, 3, :], rsc[:, 2, :],
                        rsc[:, 0, 3:4].broadcast_to([128, 16]), ALU.is_equal),
                        reads=[rr], writes=[rr], tag="f0_req1")
                    p.emit("v", lambda v: v.scalar_tensor_tensor(
                        rsc[:, 4, :], rsc[:, 3, :], -1.0e30, rsc[:, 2, :],
                        ALU.mult, ALU.add),
                        reads=[rr], writes=[rr], tag="f0_rmask")
                    p.emit("v", lambda v: v.reduce_max(
                        rsc[:, 0, 4:5], rsc[:, 4, :], axis=AX.X),
                        reads=[rr], writes=[rr], tag="f0_rm2")
                    p.emit("v", lambda v: v.tensor_tensor(
                        rsc[:, 5, :], rsc[:, 2, :],
                        rsc[:, 0, 4:5].broadcast_to([128, 16]), ALU.is_equal),
                        reads=[rr], writes=[rr], tag="f0_req2")
                    p.emit("v", lambda v: v.tensor_add(
                        rsc[:, 3, :], rsc[:, 3, :], rsc[:, 5, :]),
                        reads=[rr], writes=[rr], tag="f0_r12")
                    p.emit("v", lambda v, ti=ti: v.tensor_mul(
                        cw[:, ti, :], rsc[:, 2, :], rsc[:, 3, :]),
                        reads=[rr], writes=[rcw], tag="f0_rcw")
                    for half in range(2):
                        halfpath(half, [rwg[half]], [rwu[half]],
                                 (lambda k, half=half: wgs[:, half, k, :]),
                                 (lambda k, half=half: wus[:, half, k, :]),
                                 rxn[sl], sl)
                    def f_dn(t):
                        last = None
                        for nh in range(2):
                            for kk in range(8):
                                last = t.matmul(
                                    p_dn[:, ts(nh, 512)],
                                    hT[:, kk // 4, :].rearrange(
                                        "p (f c) -> p f c", f=4)[:, kk % 4, :],
                                    wds[:, kk // 4, kk % 4, ts(nh, 512)],
                                    start=(kk == 0), stop=(kk == 7))
                        return last
                    p.emit("t", f_dn, reads=[rhT[0], rhT[1], rwd[0], rwd[1]],
                           writes=[rp_dn], tag="f0_sdn")
                    p.emit("s", lambda s, ti=ti: s.copy(
                        acc[:, ti, :], p_dn[:, :]),
                        reads=[rp_dn], writes=[racc], tag="f0_sacc")
                for quad in range(4):
                    for qe in range(4):
                        e = quad * 4 + qe
                        slot_load(p, wgs[:, qe, :, :],
                                  inp["L0_ffn_Wg"][e].rearrange(
                                      "(kt p) n -> p kt n", p=128), rwg[qe],
                                  tag="f0_lwg")
                        slot_load(p, wus[:, qe, :, :],
                                  inp["L0_ffn_Wu"][e].rearrange(
                                      "(kt p) n -> p kt n", p=128), rwu[qe],
                                  tag="f0_lwu")
                        slot_load(p, wds[:, qe, :, :],
                                  inp["L0_ffn_Wd"][e].rearrange(
                                      "(kt p) n -> p kt n", p=128), rwd[qe],
                                  tag="f0_lwd")
                    for ti in range(NTG):
                        sl = ti % 2
                        tt = gi * NTG + ti
                        xload(tt, sl)
                        for qe in range(4):
                            e = quad * 4 + qe
                            sl2 = qe % 2
                            halfpath(sl2, [rwg[qe]], [rwu[qe]],
                                     (lambda k, qe=qe: wgs[:, qe, k, :]),
                                     (lambda k, qe=qe: wus[:, qe, k, :]),
                                     rxn[sl], sl,
                                     cw_ap=cw[:, ti, e:e + 1], rcw_l=[rcw])
                            def f_edn(t, qe=qe, sl2=sl2):
                                last = None
                                for nh in range(2):
                                    for f in range(4):
                                        last = t.matmul(
                                            p_dn[:, ts(nh, 512)],
                                            hT[:, sl2, :].rearrange(
                                                "p (f c) -> p f c",
                                                f=4)[:, f, :],
                                            wds[:, qe, f, ts(nh, 512)],
                                            start=(qe == 0 and f == 0),
                                            stop=(qe == 3 and f == 3))
                                return last
                            p.emit("t", f_edn, reads=[rhT[sl2], rwd[qe]],
                                   writes=[rp_dn], tag="f0_edn")
                        p.emit("v", lambda v, ti=ti: v.tensor_add(
                            acc[:, ti, :], acc[:, ti, :], p_dn[:, :]),
                            reads=[racc, rp_dn], writes=[racc], tag="f0_eacc")
                for ti in range(NTG):
                    sl = ti % 2
                    tt = gi * NTG + ti
                    p.emit("y", lambda y, tt=tt, sl=sl: y.dma_start(
                        out=xsb[:, sl, :], in_=x_hbm[ts(tt), :]),
                        writes=[rx[sl]], sig=16, tag="f0_xeld")
                    p.emit("y", lambda y, tt=tt, sl=sl: y.dma_start(
                        out=atb[:, sl, :], in_=attn_hbm[ts(tt), :]),
                        writes=[rat[sl]], sig=16, tag="f0_aeld")
                    p.emit("v", lambda v, sl=sl: v.tensor_add(
                        outb[:, sl, :], xsb[:, sl, :], atb[:, sl, :]),
                        reads=[rx[sl], rat[sl]], writes=[rob[sl]],
                        tag="f0_add1")
                    p.emit("v", lambda v, ti=ti, sl=sl: v.tensor_add(
                        outb[:, sl, :], outb[:, sl, :], acc[:, ti, :]),
                        reads=[rob[sl], racc], writes=[rob[sl]], tag="f0_add2")
                    p.emit("g", lambda g, tt=tt, sl=sl: g.dma_start(
                        out=xout_hbm[ts(tt), :], in_=outb[:, sl, :]),
                        reads=[rob[sl]], sig=16, tag="f0_xost")
    p.barrier(tag="l0f_end")



L0_WNAMES = ["L0_attn_Wq", "L0_attn_Wk", "L0_attn_Wv", "L0_attn_W_G",
             "L0_attn_W_O", "L0_ffn_Wr", "L0_ffn_Wg", "L0_ffn_Wu",
             "L0_ffn_Wd", "L0_ffn_Sg", "L0_ffn_Su", "L0_ffn_Sd"]
CONST_SHAPES = None


def _declare_inputs(nc, names, consts):
    inp = {}
    shapes = {
        "L0_attn_Wq": [D, D], "L0_attn_Wk": [D, D], "L0_attn_Wv": [D, D],
        "L0_attn_W_G": [D, D], "L0_attn_W_O": [D, D],
        "L0_ffn_Wr": [D, 16], "L0_ffn_Wg": [16, D, 512],
        "L0_ffn_Wu": [16, D, 512], "L0_ffn_Wd": [16, 512, D],
        "L0_ffn_Sg": [D, D], "L0_ffn_Su": [D, D], "L0_ffn_Sd": [D, D],
    }
    for n in names:
        inp[n] = nc.dram_tensor(n, shapes[n], F32, kind="ExternalInput").ap()
    for cn, arr in consts.items():
        dt = BF16 if arr.dtype == BF else F32
        inp[cn] = nc.dram_tensor(cn, list(arr.shape), dt,
                                 kind="ExternalInput").ap()
    return inp


def build_layer0(consts, debug=False):
    nc = bass.Bass("TRN2", target_bir_lowering=False)
    x_in = nc.dram_tensor("x_in", [T, D], F32, kind="ExternalInput").ap()
    x_out = nc.dram_tensor("x_out", [T, D], F32, kind="ExternalOutput").ap()
    kind = "ExternalOutput" if debug else "Internal"
    xn = nc.dram_tensor("dbg_xn", [T, D], F32, kind=kind).ap()
    xnT = nc.dram_tensor("dbg_xnT", [D, T], BF16, kind=kind).ap()
    attn = nc.dram_tensor("dbg_attn", [T, D], F32, kind=kind).ap()
    dbg = None
    if debug:
        dbg = {
            "yraw": nc.dram_tensor("dbg_yraw", [T, D], F32, kind=kind).ap(),
            "gvar": nc.dram_tensor("dbg_gvar", [T, 8], F32, kind=kind).ap(),
        }
    inp = _declare_inputs(nc, L0_WNAMES, consts)
    p = Prog(nc)
    with contextlib.ExitStack() as st0:
        phase_ln(p, st0, x_in, xn, xnT, inp)
    phase_l0_attn(p, xnT, attn, inp, dbg=dbg)
    phase_l0_ffn(p, xnT, x_in, attn, x_out, inp)
    p.build()
    return nc, p



def kernel_unfused(**inputs):
    consts = build_consts()
    builders = [build_layer0, build_layer1, build_layer2, build_layer3]
    X = np.ascontiguousarray(inputs["L0_attn_X"], dtype=np.float32)
    x_cur = [np.ascontiguousarray(X[2 * c:2 * c + 2].reshape(T, D))
             for c in range(NCORES)]
    for layer in range(4):
        nc, p = builders[layer](consts)
        in_maps = []
        for c in range(NCORES):
            m = {"x_in": x_cur[c]}
            m.update(layer_inputs(layer, inputs, consts, c))
            in_maps.append(m)
        res = run_bass_kernel_spmd(nc, in_maps, core_ids=list(range(NCORES)))
        x_cur = [np.asarray(res.results[c]["x_out"]) for c in range(NCORES)]
    out = np.concatenate(
        [x.reshape(B_PER_CORE, S, D) for x in x_cur], axis=0)
    return out.astype(np.float32)



def phase_l1_attn(p, xnT_hbm, xn_hbm, attn_hbm, inp):
    nc = p.nc
    CH = 256
    NCH_B = S // CH
    with contextlib.ExitStack() as st:
        w_in, rw_in = wload(p, st, "l1in", inp["L1_attn_in_w"], D, 2 * D)
        w_pa, rw_pa = wload(p, st, "l1pa", inp["L1_attn_param_w"], D, 3 * D)
        w_out, rw_out = wload(p, st, "l1out", inp["L1_attn_out_w"], D, D)
        nw, rnw = cload(p, st, "l1nw", inp["L1c_norm_w"], [128, KT])
        cwc, rcwc = cload(p, st, "l1cw", inp["L1c_conv_w"], [128, KT, 4])
        cbc, rcbc = cload(p, st, "l1cb", inp["L1c_conv_b"], [128, KT])
        skp, rskp = cload(p, st, "l1sk", inp["L1c_skip"], [128, KT])
        inb, rinb = cload(p, st, "l1ib", inp["L1c_in_b"], [128, 16])
        pab, rpab = cload(p, st, "l1pb", inp["L1c_param_b"], [128, 24])
        outb_r, routb = cload(p, st, "l1ob", inp["L1_attn_out_b"], [1, D], BF16)
        ones1, rones1 = cload(p, st, "l1o1", inp["c_ones1_bf"], [1, 128], BF16)
        onesc, ronesc = cload(p, st, "l1oc", inp["c_ones128_bf"], [128, 1], BF16)
        rfold = p.reg("l1_fold")
        for kt in range(KT):
            p.emit("v", lambda v, kt=kt: v.tensor_scalar_mul(
                w_in[:, kt, :], w_in[:, kt, :], nw[:, kt:kt + 1]),
                reads=[rnw], writes=[rw_in_mut] if False else [rfold],
                waits=[rw_in.last_w], tag="l1_fold")
        rw_in2 = rfold
        with contextlib.ExitStack() as es:
            xnT = sb(es, nc, "a1_xnT", [128, 2, KT, CH], BF16)
            sqb = sb(es, nc, "a1_sq", [128, KT, CH], BF16)
            rowst = sb(es, nc, "a1_row", [1, 4, CH], F32)
            rowbf = sb(es, nc, "a1_rbf", [1, CH], BF16)
            bcr = sb(es, nc, "a1_bc", [128, CH], BF16)
            xn2 = sb(es, nc, "a1_xn2", [128, KT, CH], BF16)
            ut = sb(es, nc, "a1_u", [128, KT, 4 + CH], F32)
            usil = sb(es, nc, "a1_us", [128, 2, KT, CH], BF16)
            gats = sb(es, nc, "a1_ga", [128, KT, CH], BF16)
            ag = sb(es, nc, "a1_agate", [128, KT, CH], F32)
            bg = sb(es, nc, "a1_bgate", [128, KT, CH], F32)
            cg = sb(es, nc, "a1_cgate", [128, KT, CH], F32)
            stt = sb(es, nc, "a1_state", [128, 2, KT, CH], F32)
            yt = sb(es, nc, "a1_y", [128, KT, CH], F32)
            ybf = sb(es, nc, "a1_ybf", [128, KT, CH], BF16)
            xnt_b = sb(es, nc, "a1_xnb", [128, 2, D], F32)
            osb = sb(es, nc, "a1_o", [128, 2, D], F32)
            p_a = pst(es, nc, "a1_p_a", [128, 2, 512], F32)
            p_row = pst(es, nc, "a1_p_row", [1, 512], F32)
            p_bc = pst(es, nc, "a1_p_bc", [128, 512], F32)
            p_o = pst(es, nc, "a1_p_o", [128, 2, 1024], F32)
            rxn = p.regs_n("a1_xnT", 2)
            rsqb, rrow, rbc, rxn2 = (p.reg("a1_sq"), p.reg("a1_row"),
                                     p.reg("a1_bc"), p.reg("a1_xn2"))
            rut = p.reg("a1_u")
            rusil = p.regs_n("a1_us", 2)
            rga, rag, rbg, rcg = (p.reg("a1_ga"), p.reg("a1_ag"),
                                  p.reg("a1_bg"), p.reg("a1_cg"))
            ryt, rybf = p.reg("a1_y"), p.reg("a1_ybf")
            rcv = ryt
            cv = yt
            bu = bg
            rbu = rbg
            rstt = p.regs_n("a1_st", 2)
            rxnb = p.regs_n("a1_xnb", 2)
            rosb = p.regs_n("a1_o", 2)
            rp_a = p.pregs_n("a1_p_a", 2)
            rp_row, rp_bc = p.preg("a1_p_row"), p.preg("a1_p_bc")
            rp_o = p.pregs_n("a1_p_o", 2)
            xnTr = xnT_hbm.rearrange("(kt r) t -> r kt t", r=128)
            for bi in range(B_PER_CORE):
                for ci in range(NCH_B):
                    sl = ci % 2
                    t0 = bi * S + ci * CH
                    p.emit("y", lambda y, sl=sl, t0=t0: y.dma_start(
                        out=xnT[:, sl, :, :], in_=xnTr[:, :, t0:t0 + CH]),
                        writes=[rxn[sl]], tag="a1_xld")
                    p.emit("s", lambda s, sl=sl: s.activation(
                        sqb[:, :, :], xnT[:, sl, :, :], AF.Square),
                        reads=[rxn[sl]], writes=[rsqb], tag="a1_sq")
                    def f_ssq(t):
                        last = None
                        for kt in range(KT):
                            last = t.matmul(p_row[0:1, 0:CH], onesc[:, :],
                                            sqb[:, kt, :],
                                            start=(kt == 0), stop=(kt == KT - 1))
                        return last
                    p.emit("t", f_ssq, reads=[rsqb, ronesc], writes=[rp_row],
                           tag="a1_ssq")
                    p.emit("v", lambda v: v.tensor_scalar(
                        rowst[0:1, 0, :], p_row[0:1, 0:CH], 1.0 / D, 1e-6,
                        ALU.mult, ALU.add),
                        reads=[rp_row], writes=[rrow], tag="a1_ms")
                    p.emit("s", lambda s: s.activation(
                        rowst[0:1, 1, :], rowst[0:1, 0, :], AF.Sqrt),
                        reads=[rrow], writes=[rrow], tag="a1_srt")
                    p.emit("v", lambda v: v.reciprocal(
                        rowst[0:1, 2, :], rowst[0:1, 1, :]),
                        reads=[rrow], writes=[rrow], tag="a1_rc")
                    p.emit("v", lambda v: v.tensor_copy(
                        rowbf[0:1, :], rowst[0:1, 2, :]),
                        reads=[rrow], writes=[rrow], tag="a1_rcb")
                    p.emit("t", lambda t: t.matmul(
                        p_bc[:, 0:CH], ones1[:, :], rowbf[0:1, :],
                        start=True, stop=True),
                        reads=[rrow, rones1], writes=[rp_bc], tag="a1_bcm")
                    p.emit("v", lambda v: v.tensor_copy(bcr[:, :], p_bc[:, 0:CH]),
                           reads=[rp_bc], writes=[rbc], tag="a1_bce")
                    p.emit("v", lambda v, sl=sl: v.tensor_tensor(
                        xn2[:, :, :], xnT[:, sl, :, :],
                        bcr[:, :].rearrange("p (o c) -> p o c", o=1).broadcast_to(
                            [128, KT, CH]), ALU.mult),
                        reads=[rxn[sl], rbc], writes=[rxn2], tag="a1_xn2")
                    for n in range(16):
                        s2 = n % 2
                        def f_pj(t, n=n, s2=s2):
                            last = None
                            for k in range(KT):
                                last = t.matmul(p_a[:, s2, 0:CH],
                                                w_in[:, k, ts(n)],
                                                xn2[:, k, :],
                                                start=(k == 0), stop=(k == KT - 1))
                            return last
                        p.emit("t", f_pj, reads=[rxn2, rw_in2], writes=[rp_a[s2]],
                               tag="a1_pj")
                        if n < KT:
                            p.emit("s", lambda s, n=n, s2=s2: s.activation(
                                ut[:, n, 4:4 + CH], p_a[:, s2, 0:CH], AF.Identity,
                                bias=inb[:, n:n + 1]),
                                reads=[rp_a[s2], rinb], writes=[rut], tag="a1_ue")
                        else:
                            p.emit("s", lambda s, n=n, s2=s2: s.activation(
                                gats[:, n - KT, :], p_a[:, s2, 0:CH], AF.Silu,
                                bias=inb[:, n:n + 1]),
                                reads=[rp_a[s2], rinb], writes=[rga], tag="a1_ge")
                    if ci == 0:
                        p.emit("v", lambda v: v.memset(ut[:, :, 0:4], 0.0),
                               writes=[rut], tag="a1_tz")
                    for kt in range(KT):
                        p.emit("v", lambda v, kt=kt: v.tensor_scalar(
                            cv[:, kt, :], ut[:, kt, 4:4 + CH], cwc[:, kt, 3:4],
                            cbc[:, kt:kt + 1], ALU.mult, ALU.add),
                            reads=[rut, rcwc, rcbc], writes=[rcv], tag="a1_c3")
                        for k in range(3):
                            p.emit("v", lambda v, kt=kt, k=k:
                                   v.scalar_tensor_tensor(
                                       cv[:, kt, :],
                                       ut[:, kt, k + 1:k + 1 + CH],
                                       cwc[:, kt, k:k + 1], cv[:, kt, :],
                                       ALU.mult, ALU.add),
                                   reads=[rut, rcwc, rcv], writes=[rcv],
                                   tag="a1_ct")
                    p.emit("v", lambda v: v.tensor_copy(
                        ut[:, :, 0:4], ut[:, :, CH:CH + 4]),
                        reads=[rut], writes=[rut], tag="a1_tl")
                    p.emit("s", lambda s, sl=sl: s.activation(
                        usil[:, sl, :, :], cv[:, :, :], AF.Silu),
                        reads=[rcv], writes=[rusil[sl]], tag="a1_usil")
                    for n in range(24):
                        s2 = n % 2
                        def f_pp(t, n=n, s2=s2, sl=sl):
                            last = None
                            for k in range(KT):
                                last = t.matmul(p_a[:, s2, 0:CH],
                                                w_pa[:, k, ts(n)],
                                                usil[:, sl, k, :],
                                                start=(k == 0), stop=(k == KT - 1))
                            return last
                        p.emit("t", f_pp, reads=[rusil[sl], rw_pa],
                               writes=[rp_a[s2]], tag="a1_pp")
                        dst, rdst, fun = (
                            (ag, rag, AF.Sigmoid) if n < KT else
                            (bg, rbg, AF.Tanh) if n < 2 * KT else
                            (cg, rcg, AF.Sigmoid))
                        p.emit("s", lambda s, n=n, s2=s2, dst=dst, fun=fun:
                               s.activation(dst[:, n % KT, :],
                                            p_a[:, s2, 0:CH],
                                            fun, bias=pab[:, n:n + 1]),
                               reads=[rp_a[s2], rpab], writes=[rdst], tag="a1_pe")
                    p.emit("v", lambda v, sl=sl: v.tensor_tensor(
                        bu[:, :, :], bg[:, :, :], usil[:, sl, :, :], ALU.mult),
                        reads=[rusil[sl]], writes=[rbu], tag="a1_bu")
                    for kt in range(KT):
                        ini = 0.0 if ci == 0 else stt[:, 1 - sl, kt, CH - 1:CH]
                        p.emit("v", lambda v, kt=kt, sl=sl, ini=ini:
                               v.tensor_tensor_scan(
                                   stt[:, sl, kt, :], ag[:, kt, :], bu[:, kt, :],
                                   ini, ALU.mult, ALU.add),
                               reads=[rag, rbu] + ([rstt[1 - sl]] if ci else []),
                               writes=[rstt[sl]], tag="a1_scan")
                    p.emit("v", lambda v, sl=sl: v.tensor_tensor(
                        yt[:, :, :], cg[:, :, :], stt[:, sl, :, :], ALU.mult),
                        reads=[rcg, rstt[sl]], writes=[ryt], tag="a1_cs")
                    for kt in range(KT):
                        p.emit("v", lambda v, kt=kt, sl=sl: v.scalar_tensor_tensor(
                            yt[:, kt, :], usil[:, sl, kt, :], skp[:, kt:kt + 1],
                            yt[:, kt, :], ALU.mult, ALU.add),
                            reads=[rusil[sl], rskp, ryt], writes=[ryt],
                            tag="a1_sku")
                    p.emit("v", lambda v: v.tensor_tensor(
                        ybf[:, :, :], yt[:, :, :], gats[:, :, :], ALU.mult),
                        reads=[ryt, rga], writes=[rybf], tag="a1_yg")
                    for tt in range(CH // 128):
                        gtile = (t0 + tt * 128) // 128
                        s2 = tt % 2
                        p.emit("y", lambda y, gtile=gtile, s2=s2: y.dma_start(
                            out=xnt_b[:, s2, :], in_=xn_hbm[ts(gtile), :]),
                            writes=[rxnb[s2]], tag="a1_xnld")
                        def f_out2(t, tt=tt, s2=s2):
                            last = None
                            for half in range(2):
                                last = t.matmul(p_o[:, s2, ts(half, 512)],
                                                ones1[:, :],
                                                outb_r[0:1, ts(half, 512)],
                                                start=True, stop=False)
                                for k in range(KT):
                                    last = t.matmul(
                                        p_o[:, s2, ts(half, 512)],
                                        ybf[:, k, ts(tt)],
                                        w_out[:, k, ts(half, 512)],
                                        start=False, stop=(k == KT - 1))
                            return last
                        p.emit("t", f_out2, reads=[rybf, rw_out, routb, rones1],
                               writes=[rp_o[s2]], tag="a1_omm")
                        p.emit("v", lambda v, s2=s2: v.tensor_add(
                            osb[:, s2, :], p_o[:, s2, :], xnt_b[:, s2, :]),
                            reads=[rp_o[s2], rxnb[s2]], writes=[rosb[s2]],
                            tag="a1_oadd")
                        p.emit("g", lambda g, gtile=gtile, s2=s2: g.dma_start(
                            out=attn_hbm[ts(gtile), :], in_=osb[:, s2, :]),
                            reads=[rosb[s2]], tag="a1_ost")
    p.barrier(tag="l1a_end")



def phase_l1_ffn(p, xnT_hbm, x_hbm, attn_hbm, xout_hbm, inp):
    nc = p.nc
    E1 = 8
    TF = 2048
    NTF = TF // 128
    xnTr = xnT_hbm.rearrange("(kt r) t -> r kt t", r=128)
    with contextlib.ExitStack() as st:
        iota8, riota = cload(p, st, "l1iota", inp["c_iota8"], [128, 8])
        ones1, rones1 = cload(p, st, "l1fo1", inp["c_ones1_bf"], [1, 128], BF16)
        idb, ridb = cload(p, st, "l1fid", inp["c_ident_bf"], [128, 128], BF16)
        idx_i, ridx = cload(p, st, "l1idx", inp["L1_ffn_router_indices"],
                            [128, NTF, 2], I32)
        rw_i, rrw = cload(p, st, "l1rw", inp["L1_ffn_routing_weights"],
                          [128, NTF, 8])

        with contextlib.ExitStack() as es:
            wgu = sb(es, nc, "f1_wgu", [128, 1, KT, 2 * D], BF16)
            wdn = sb(es, nc, "f1_wdn", [128, 1, KT, D], BF16)
            acc = sb(es, nc, "f1_acc", [128, NTF, D], F32)
            cwt = sb(es, nc, "f1_cw", [128, NTF, E1], F32)
            idxf = sb(es, nc, "f1_idxf", [128, NTF, 2], F32)
            scr = sb(es, nc, "f1_scr", [128, 8], F32)
            xnT = sb(es, nc, "f1_xnT", [128, 2, KT, 128], BF16)
            gc = sb(es, nc, "f1_gc", [128, 2, D], BF16)
            sg = sb(es, nc, "f1_sg", [128, 2, D], BF16)
            up1 = sb(es, nc, "f1_up", [128, 2, D], BF16)
            itr = sb(es, nc, "f1_it", [128, 2, D], BF16)
            itT = sb(es, nc, "f1_itT", [128, 2, D], BF16)
            xsb = sb(es, nc, "f1_x", [128, 2, D], F32)
            atb = sb(es, nc, "f1_at", [128, 2, D], F32)
            outb = sb(es, nc, "f1_o", [128, 2, D], F32)
            gub_r = sb(es, nc, "f1_gube", [1, 2 * D], BF16)
            dnb_r = sb(es, nc, "f1_dnbe", [1, D], BF16)
            p_gu = pst(es, nc, "f1_p_gu", [128, 2048], F32)
            p_dn = pst(es, nc, "f1_p_dn", [128, 1024], F32)
            p_tp = pst(es, nc, "f1_p_tp", [128, 2, 1024], BF16)
            rwgu = p.regs_n("f1_wgu", 1)
            rwdn = p.regs_n("f1_wdn", 1)
            racc, rcw, ridxf, rscr = (p.reg("f1_acc"), p.reg("f1_cw"),
                                      p.reg("f1_idxf"), p.reg("f1_scr"))
            rxn = p.regs_n("f1_xnT", 2)
            rgc = p.regs_n("f1_gc", 2)
            rsg = p.regs_n("f1_sg", 2)
            rup = p.regs_n("f1_up", 2)
            rit = p.regs_n("f1_it", 2)
            ritT = p.regs_n("f1_itT", 2)
            rx = p.regs_n("f1_x", 2)
            rat = p.regs_n("f1_at", 2)
            rob = p.regs_n("f1_o", 2)
            rgub = p.reg("f1_gube")
            rdnb = p.reg("f1_dnbe")
            rp_gu = p.preg("f1_p_gu")
            rp_dn = p.preg("f1_p_dn")
            rp_tp = p.pregs_n("f1_p_tp", 2)
            p.emit("v", lambda v: v.tensor_copy(idxf[...], idx_i[...]),
                   reads=[ridx], writes=[ridxf], tag="f1_cast")
            p.emit("v", lambda v: v.memset(cwt[...], 0.0), writes=[rcw],
                   tag="f1_cw0")
            for ti in range(NTF):
                for j in range(2):
                    p.emit("v", lambda v, ti=ti, j=j: v.tensor_tensor(
                        scr[:, :], idxf[:, ti, j:j + 1].broadcast_to([128, 8]),
                        iota8[:, :], ALU.is_equal),
                        reads=[ridxf, riota], writes=[rscr], tag="f1_eq")
                    p.emit("v", lambda v, ti=ti: v.tensor_tensor(
                        scr[:, :], scr[:, :], rw_i[:, ti, :], ALU.mult),
                        reads=[rscr, rrw], writes=[rscr], tag="f1_eqw")
                    p.emit("v", lambda v, ti=ti: v.tensor_add(
                        cwt[:, ti, :], cwt[:, ti, :], scr[:, :]),
                        reads=[rscr, rcw], writes=[rcw], tag="f1_cwa")
            p.emit("v", lambda v: v.memset(acc[...], 0.0), writes=[racc],
                   tag="f1_acc0")
            for e in range(E1):
                esl = 0
                slot_load(p, wgu[:, esl, :, :],
                          inp["L1f_gu"][e].rearrange("(kt pp) n -> pp kt n",
                                                     pp=128),
                          rwgu[esl], tag="f1_lgu")
                slot_load(p, wdn[:, esl, :, :],
                          inp["L1_ffn_down_proj"][e].rearrange(
                              "(kt pp) n -> pp kt n", pp=128),
                          rwdn[esl], tag="f1_ldn")
                slot_load(p, gub_r[0:1, :], inp["L1f_gu_bias"][0:1, e, :],
                          rgub, tag="f1_lgub")
                slot_load(p, dnb_r[0:1, :], inp["L1f_dn_bias"][0:1, e, :],
                          rdnb, tag="f1_ldnb")
                for ti in range(NTF):
                    sl = ti % 2
                    p.emit("y", lambda y, ti=ti, sl=sl: y.dma_start(
                        out=xnT[:, sl, :, :], in_=xnTr[:, :, ts(ti)]),
                        writes=[rxn[sl]], tag="f1_xld")
                    def f_gu(t, sl=sl, esl=esl, e=e):
                        last = None
                        for nq in range(4):
                            last = t.matmul(p_gu[:, ts(nq, 512)], ones1[:, :],
                                            gub_r[0:1, ts(nq, 512)],
                                            start=True, stop=False)
                            for k in range(KT):
                                last = t.matmul(p_gu[:, ts(nq, 512)],
                                                xnT[:, sl, k, :],
                                                wgu[:, esl, k, ts(nq, 512)],
                                                start=False, stop=(k == KT - 1))
                        return last
                    p.emit("t", f_gu, reads=[rxn[sl], rwgu[esl], rgub, rones1],
                           writes=[rp_gu], tag="f1_gum")
                    p.emit("v", lambda v, sl=sl: v.tensor_scalar_min(
                        gc[:, sl, :], p_gu[:, 0:D], 7.0),
                        reads=[rp_gu], writes=[rgc[sl]], tag="f1_gc")
                    p.emit("s", lambda s, sl=sl: s.activation(
                        sg[:, sl, :], gc[:, sl, :], AF.Sigmoid, scale=1.702),
                        reads=[rgc[sl]], writes=[rsg[sl]], tag="f1_sg")
                    p.emit("v", lambda v, sl=sl: v.tensor_scalar(
                        up1[:, sl, :], p_gu[:, D:2 * D], 7.0, -7.0,
                        ALU.min, ALU.max),
                        reads=[rp_gu], writes=[rup[sl]], tag="f1_up")
                    p.emit("v", lambda v, sl=sl: v.tensor_tensor(
                        gc[:, sl, :], gc[:, sl, :], sg[:, sl, :], ALU.mult),
                        reads=[rgc[sl], rsg[sl]], writes=[rgc[sl]], tag="f1_glu")
                    p.emit("v", lambda v, sl=sl: v.scalar_tensor_tensor(
                        itr[:, sl, :], up1[:, sl, :], 1.0, gc[:, sl, :],
                        ALU.add, ALU.mult),
                        reads=[rup[sl], rgc[sl]], writes=[rit[sl]], tag="f1_it")
                    def f_tp(t, sl=sl):
                        last = None
                        for j in range(KT):
                            last = t.transpose(p_tp[:, sl, ts(j)],
                                               itr[:, sl, ts(j)], idb[:, :])
                        return last
                    p.emit("t", f_tp, reads=[rit[sl], ridb], writes=[rp_tp[sl]],
                           tag="f1_tp")
                    p.emit("v", lambda v, sl=sl: v.tensor_copy(
                        itT[:, sl, :], p_tp[:, sl, :]),
                        reads=[rp_tp[sl]], writes=[ritT[sl]], tag="f1_tpe")
                    def f_dn(t, sl=sl, esl=esl, e=e):
                        last = None
                        for nh in range(2):
                            last = t.matmul(p_dn[:, ts(nh, 512)], ones1[:, :],
                                            dnb_r[0:1, ts(nh, 512)],
                                            start=True, stop=False)
                            for k in range(KT):
                                last = t.matmul(
                                    p_dn[:, ts(nh, 512)],
                                    itT[:, sl, ts(k)],
                                    wdn[:, esl, k, ts(nh, 512)],
                                    start=False, stop=(k == KT - 1))
                        return last
                    p.emit("t", f_dn, reads=[ritT[sl], rwdn[esl], rdnb, rones1],
                           writes=[rp_dn], tag="f1_dnm")
                    p.emit("v", lambda v, ti=ti, e=e: v.scalar_tensor_tensor(
                        acc[:, ti, :], p_dn[:, :], cwt[:, ti, e:e + 1],
                        acc[:, ti, :], ALU.mult, ALU.add),
                        reads=[rp_dn, rcw, racc], writes=[racc], tag="f1_acc")
            for tt2 in range(NTILE):
                sl = tt2 % 2
                p.emit("y", lambda y, tt2=tt2, sl=sl: y.dma_start(
                    out=xsb[:, sl, :], in_=x_hbm[ts(tt2), :]),
                    writes=[rx[sl]], tag="f1_xeld")
                p.emit("y", lambda y, tt2=tt2, sl=sl: y.dma_start(
                    out=atb[:, sl, :], in_=attn_hbm[ts(tt2), :]),
                    writes=[rat[sl]], tag="f1_aeld")
                p.emit("v", lambda v, sl=sl: v.tensor_add(
                    outb[:, sl, :], xsb[:, sl, :], atb[:, sl, :]),
                    reads=[rx[sl], rat[sl]], writes=[rob[sl]], tag="f1_add1")
                if tt2 < NTF:
                    p.emit("v", lambda v, tt2=tt2, sl=sl: v.tensor_add(
                        outb[:, sl, :], outb[:, sl, :], acc[:, tt2, :]),
                        reads=[rob[sl], racc], writes=[rob[sl]], tag="f1_add2")
                p.emit("g", lambda g, tt2=tt2, sl=sl: g.dma_start(
                    out=xout_hbm[ts(tt2), :], in_=outb[:, sl, :]),
                    reads=[rob[sl]], tag="f1_xost")
    p.barrier(tag="l1f_end")


L1_WNAMES = ["L1_attn_in_w", "L1_attn_param_w", "L1_attn_out_w",
             "L1_ffn_down_proj"]
L1_DERIVED = ["L1c_norm_w", "L1c_conv_w", "L1c_conv_b", "L1c_skip",
              "L1c_in_b", "L1c_param_b", "L1_attn_out_b",
              "L1_ffn_router_indices", "L1_ffn_routing_weights",
              "L1f_gu", "L1f_gu_bias", "L1f_dn_bias"]


def l1_host_relayout(inputs, core_id):
    d = {}
    f = lambda a: np.ascontiguousarray(a, dtype=np.float32)
    d["L1c_norm_w"] = f(inputs["L1_attn_norm_w"]).reshape(KT, 128).T.copy()
    d["L1c_conv_w"] = f(inputs["L1_attn_conv_w"]).T.reshape(KT, 128, 4
                                                            ).transpose(1, 0, 2).copy()
    d["L1c_conv_b"] = f(inputs["L1_attn_conv_b"]).reshape(KT, 128).T.copy()
    d["L1c_skip"] = f(inputs["L1_attn_skip"]).reshape(KT, 128).T.copy()
    d["L1c_in_b"] = f(inputs["L1_attn_in_b"]).reshape(16, 128).T.copy()
    d["L1c_param_b"] = f(inputs["L1_attn_param_b"]).reshape(24, 128).T.copy()
    d["L1_attn_out_b"] = f(inputs["L1_attn_out_b"]).reshape(1, D)
    d["L1_attn_in_w"] = f(inputs["L1_attn_in_w"])
    d["L1_attn_param_w"] = f(inputs["L1_attn_param_w"])
    d["L1_attn_out_w"] = f(inputs["L1_attn_out_w"])
    idx = np.ascontiguousarray(inputs["L1_ffn_router_indices"], np.int32)
    d["L1_ffn_router_indices"] = idx.reshape(16, 128, 2).transpose(1, 0, 2).copy()
    rw = f(inputs["L1_ffn_routing_weights"])
    if core_id != 0:
        rw = np.zeros_like(rw)
    d["L1_ffn_routing_weights"] = rw.reshape(16, 128, 8).transpose(1, 0, 2).copy()
    gu = f(inputs["L1_ffn_gate_up_proj"])
    d["L1f_gu"] = np.concatenate([gu[..., 0::2], gu[..., 1::2]], axis=-1).copy()
    gub = f(inputs["L1_ffn_gate_up_proj_bias"])
    d["L1f_gu_bias"] = np.ascontiguousarray(np.concatenate(
        [gub[:, 0::2], gub[:, 1::2]], axis=-1).reshape(1, 8, 2 * D))
    d["L1_ffn_down_proj"] = f(inputs["L1_ffn_down_proj"])
    d["L1f_dn_bias"] = f(inputs["L1_ffn_down_proj_bias"]).reshape(1, 8, D)
    return d


def build_layer1(consts, debug=False):
    nc = bass.Bass("TRN2", target_bir_lowering=False)
    x_in = nc.dram_tensor("x_in", [T, D], F32, kind="ExternalInput").ap()
    x_out = nc.dram_tensor("x_out", [T, D], F32, kind="ExternalOutput").ap()
    kind = "ExternalOutput" if debug else "Internal"
    xn = nc.dram_tensor("dbg_xn", [T, D], F32, kind=kind).ap()
    xnT = nc.dram_tensor("dbg_xnT", [D, T], BF16, kind=kind).ap()
    attn = nc.dram_tensor("dbg_attn", [T, D], F32, kind=kind).ap()
    shapes = {
        "L1_attn_in_w": ([D, 2 * D], F32),
        "L1_attn_param_w": ([D, 3 * D], F32),
        "L1_attn_out_w": ([D, D], F32),
        "L1_ffn_down_proj": ([8, D, D], F32),
        "L1c_norm_w": ([128, KT], F32), "L1c_conv_w": ([128, KT, 4], F32),
        "L1c_conv_b": ([128, KT], F32), "L1c_skip": ([128, KT], F32),
        "L1c_in_b": ([128, 16], F32), "L1c_param_b": ([128, 24], F32),
        "L1_attn_out_b": ([1, D], F32),
        "L1_ffn_router_indices": ([128, 16, 2], I32),
        "L1_ffn_routing_weights": ([128, 16, 8], F32),
        "L1f_gu": ([8, D, 2 * D], F32),
        "L1f_gu_bias": ([1, 8, 2 * D], F32),
        "L1f_dn_bias": ([1, 8, D], F32),
    }
    inp = {}
    for n, (shp, dt) in shapes.items():
        inp[n] = nc.dram_tensor(n, shp, dt, kind="ExternalInput").ap()
    for cn, arr in consts.items():
        dt = BF16 if arr.dtype == BF else (I32 if arr.dtype == np.int32 else F32)
        inp[cn] = nc.dram_tensor(cn, list(arr.shape), dt,
                                 kind="ExternalInput").ap()
    p = Prog(nc)
    with contextlib.ExitStack() as st0:
        phase_ln(p, st0, x_in, xn, xnT, inp)
    phase_l1_attn(p, xnT, xn, attn, inp)
    phase_l1_ffn(p, xnT, x_in, attn, x_out, inp)
    p.build()
    return nc, p


def layer_inputs(layer, inputs, consts, core_id):
    m = {}
    if layer == 0:
        for n in L0_WNAMES:
            m[n] = np.ascontiguousarray(inputs[n], np.float32)
    elif layer == 1:
        m.update(l1_host_relayout(inputs, core_id))
    elif layer == 2:
        m.update(l2_host_relayout(inputs, core_id))
    elif layer == 3:
        m.update(l3_host_relayout(inputs, core_id))
    for cn, arr in consts.items():
        m[cn] = arr
    return m



def phase_l2_attn(p, xnT_hbm, attn_hbm, inp):
    nc = p.nc
    NTB = S // 128
    with contextlib.ExitStack() as st:
        wq, rwq = wload(p, st, "l2wq", inp["L2_attn_Wq"], D, D)
        bq, rbq = cload(p, st, "l2bq", inp["L2c_bq"], [128, KT])
        idb, ridb = cload(p, st, "l2id", inp["c_ident_bf"], [128, 128], BF16)
        with contextlib.ExitStack() as es:
            xnT = sb(es, nc, "a2_xnT", [128, 1, KT, 512], BF16)
            qT = sb(es, nc, "a2_qT", [128, KT, S], BF16)
            qB = sb(es, nc, "a2_qB", [128, NTB, D], BF16)
            ssb = sb(es, nc, "a2_s", [128, 1, S], F32)
            pbf = sb(es, nc, "a2_p", [128, 2, S], BF16)
            pT = sb(es, nc, "a2_pT", [128, 1, NTB, 128], BF16)
            rowv = sb(es, nc, "a2_row", [128, 2, 8], F32)
            osb = sb(es, nc, "a2_o", [128, 2, D], F32)
            p_q = pst(es, nc, "a2_p_q", [128, 2, 512], F32)
            p_tp = pst(es, nc, "a2_p_tp", [128, 2, 1024], BF16)
            p_s = pst(es, nc, "a2_p_s", [128, 2, 512], F32)
            p_o = pst(es, nc, "a2_p_o", [128, 1024], F32)
            rxn = p.regs_n("a2_xnT", 1) * 2
            rqT, rqB = p.reg("a2_qT"), p.reg("a2_qB")
            rs = p.regs_n("a2_s", 1) * 2
            rp = p.regs_n("a2_p", 2)
            rpT = p.regs_n("a2_pT", 1) * 2
            rrow = p.regs_n("a2_row", 2)
            ro = p.regs_n("a2_o", 2)
            rp_q = p.pregs_n("a2_p_q", 2)
            rp_tp = p.pregs_n("a2_p_tp", 2)
            rp_s = p.pregs_n("a2_p_s", 2)
            rp_o = p.preg("a2_p_o")
            xnTr = xnT_hbm.rearrange("(kt r) t -> r kt t", r=128)
            for bi in range(B_PER_CORE):
                for cc in range(S // 512):
                    sl = cc % 2
                    t0 = bi * S + cc * 512
                    p.emit("y", lambda y, t0=t0: y.dma_start(
                        out=xnT[:, 0, :, :], in_=xnTr[:, :, t0:t0 + 512]),
                        writes=[rxn[0]], tag="a2_xld")
                    for n in range(KT):
                        s2 = n % 2
                        def f_q(t, n=n, s2=s2):
                            last = None
                            for k in range(KT):
                                last = t.matmul(p_q[:, s2, :], wq[:, k, ts(n)],
                                                xnT[:, 0, k, :],
                                                start=(k == 0), stop=(k == KT - 1))
                            return last
                        p.emit("t", f_q, reads=[rxn[0], rwq], writes=[rp_q[s2]],
                               tag="a2_qmm")
                        p.emit("s", lambda s, n=n, s2=s2, cc=cc: s.activation(
                            qT[:, n, cc * 512:(cc + 1) * 512], p_q[:, s2, :],
                            AF.Identity, bias=bq[:, n:n + 1]),
                            reads=[rp_q[s2], rbq], writes=[rqT], tag="a2_qev")
                for tt in range(NTB):
                    s2 = tt % 2
                    def f_tp(t, tt=tt, s2=s2):
                        last = None
                        for j in range(KT):
                            last = t.transpose(p_tp[:, s2, ts(j)],
                                               qT[:, j, ts(tt)], idb[:, :])
                        return last
                    p.emit("t", f_tp, reads=[rqT, ridb], writes=[rp_tp[s2]],
                           tag="a2_qtp")
                    p.emit("v", lambda v, tt=tt, s2=s2: v.tensor_copy(
                        qB[:, tt, :], p_tp[:, s2, :]),
                        reads=[rp_tp[s2]], writes=[rqB], tag="a2_qte")
                for tt in range(NTB):
                    sl = tt % 2
                    for kc in range(8):
                        s2 = kc % 2
                        def f_s(t, tt=tt, kc=kc, s2=s2):
                            last = None
                            for k in range(KT):
                                last = t.matmul(
                                    p_s[:, s2, :], qT[:, k, ts(tt)],
                                    qT[:, k, kc * 512:(kc + 1) * 512],
                                    start=(k == 0), stop=(k == KT - 1))
                            return last
                        p.emit("t", f_s, reads=[rqT], writes=[rp_s[s2]],
                               tag="a2_smm")
                        eng = "s" if kc % 2 == 0 else "v"
                        fn = (lambda e, kc=kc, s2=s2, sl=sl: e.mul(
                            ssb[:, 0, kc * 512:(kc + 1) * 512],
                            p_s[:, s2, :], 1.0 / 32.0)) if eng == "s" else (
                            lambda e, kc=kc, s2=s2, sl=sl: e.tensor_scalar_mul(
                                ssb[:, 0, kc * 512:(kc + 1) * 512],
                                p_s[:, s2, :], 1.0 / 32.0))
                        p.emit(eng, fn, reads=[rp_s[s2]], writes=[rs[sl]],
                               tag="a2_sev")
                    p.emit("v", lambda v, sl=sl: v.reduce_max(
                        rowv[:, sl, 0:1], ssb[:, 0, :], axis=AX.X,
                        negate=True),
                        reads=[rs[sl]], writes=[rrow[sl]], tag="a2_max")
                    p.emit("s", lambda s, sl=sl: s.activation(
                        pbf[:, sl, :], ssb[:, 0, :], AF.Exp,
                        bias=rowv[:, sl, 0:1], accum_out=rowv[:, sl, 1:2]),
                        reads=[rs[sl], rrow[sl]], writes=[rp[sl], rrow[sl]],
                        tag="a2_exp")
                    p.emit("v", lambda v, sl=sl: v.reciprocal(
                        rowv[:, sl, 2:3], rowv[:, sl, 1:2]),
                        reads=[rrow[sl]], writes=[rrow[sl]], tag="a2_rec")
                    for g4 in range(4):
                        s2 = g4 % 2
                        def f_ptp(t, g4=g4, s2=s2, sl=sl):
                            last = None
                            for j in range(8):
                                last = t.transpose(
                                    p_tp[:, s2, ts(j)],
                                    pbf[:, sl, (g4 * 8 + j) * 128:
                                        (g4 * 8 + j + 1) * 128],
                                    idb[:, :])
                            return last
                        p.emit("t", f_ptp, reads=[rp[sl], ridb],
                               writes=[rp_tp[s2]], tag="a2_ptp")
                        eng = "v" if g4 % 2 == 0 else "s"
                        fn = (lambda e, g4=g4, s2=s2, sl=sl: e.tensor_copy(
                            pT[:, 0, g4 * 8:(g4 + 1) * 8, :].rearrange(
                                "p a b -> p (a b)"), p_tp[:, s2, :])
                        ) if eng == "v" else (
                            lambda e, g4=g4, s2=s2, sl=sl: e.copy(
                                pT[:, 0, g4 * 8:(g4 + 1) * 8, :].rearrange(
                                    "p a b -> p (a b)"), p_tp[:, s2, :]))
                        p.emit(eng, fn, reads=[rp_tp[s2]], writes=[rpT[sl]],
                               tag="a2_pte")
                    def f_o(t):
                        last = None
                        for half in range(2):
                            for k in range(NTB):
                                last = t.matmul(
                                    p_o[:, ts(half, 512)], pT[:, 0, k, :],
                                    qB[:, k, ts(half, 512)],
                                    start=(k == 0), stop=(k == NTB - 1))
                        return last
                    p.emit("t", f_o, reads=[rpT[sl], rqB], writes=[rp_o],
                           tag="a2_omm")
                    p.emit("v", lambda v, sl=sl: v.tensor_scalar_mul(
                        osb[:, sl, :], p_o[:, :], rowv[:, sl, 2:3]),
                        reads=[rp_o, rrow[sl]], writes=[ro[sl]],
                        tag="a2_oev")
                    p.emit("g", lambda g, bi=bi, tt=tt, sl=sl: g.dma_start(
                        out=attn_hbm[ts(bi * NTB + tt), :], in_=osb[:, sl, :]),
                        reads=[ro[sl]], tag="a2_ost")
    p.barrier(tag="l2a_end")



def _l2_shift_load(p, nc, dst_ap, rdst, src_hbm, gtile, bi, tag):
    batch_start = (gtile % (S // 128) == 0)
    p.emit("y", lambda y: y.dma_start(
        out=dst_ap[:, 0:512], in_=src_hbm[ts(gtile), 0:512]),
        writes=[rdst], tag=tag + "_a")
    if batch_start:
        p.emit("v", lambda v: v.memset(dst_ap[0:1, 512:1024], 0.0),
               writes=[rdst], tag=tag + "_z")
        p.emit("y", lambda y: y.dma_start(
            out=dst_ap[1:128, 512:1024],
            in_=src_hbm[gtile * 128:gtile * 128 + 127, 512:1024]),
            writes=[rdst], tag=tag + "_b")
    else:
        p.emit("y", lambda y: y.dma_start(
            out=dst_ap[:, 512:1024],
            in_=src_hbm[gtile * 128 - 1:gtile * 128 + 127, 512:1024]),
            writes=[rdst], tag=tag + "_b")


def phase_l2_ffn_attn(p, rsrc_hbm, rtgt_hbm, inp, li):
    nc = p.nc
    NTB = S // 128
    KR = 6
    L2BIS = int(os.environ.get("L2BIS", "0"))
    with contextlib.ExitStack() as st:
        wqkv, rwqkv = wload(p, st, f"l2qkv{li}", inp["L2_ffn_wqkv"][li],
                            D, 1536)
        wo_, rwo = wload(p, st, f"l2wo{li}", inp["L2_ffn_wo"][li], 512, D)
        qs_t, rqs = cload(p, st, f"l2qs{li}", inp["L2c_qs"][li], [128, 512])
        ks_t, rks = cload(p, st, f"l2ks{li}", inp["L2c_ks"][li], [128, 512])
        m0, rm0 = cload(p, st, f"l2m0_{li}", inp["c_l2_m0"], [128, 128])
        m4, rm4 = cload(p, st, f"l2m4_{li}", inp["c_l2_m4"], [128, 128])
        idb, ridb = cload(p, st, f"l2fid{li}", inp["c_ident_bf"], [128, 128],
                          BF16)
        onesc, ronesc = cload(p, st, f"l2oc{li}", inp["c_ones128_bf"],
                              [128, 1], BF16)
        with contextlib.ExitStack() as es:
            qTh = sb(es, nc, "fa_qT", [128, 4, 2, 128], BF16)
            kTh = sb(es, nc, "fa_kT", [128, 4, KR, 128], BF16)
            vB = sb(es, nc, "fa_vB", [128, KR, 512], BF16)
            oB = sb(es, nc, "fa_oB", [128, 2, 512], BF16)
            ht = sb(es, nc, "fa_h", [128, 2, D], F32)
            hbf = sb(es, nc, "fa_hb", [128, 2, D], BF16)
            hT = sb(es, nc, "fa_hT", [128, 2, KT, 128], BF16)
            qf = sb(es, nc, "fa_qf", [128, 2, 512], F32)
            kf = sb(es, nc, "fa_kf", [128, 2, 512], F32)
            sqs_ = sb(es, nc, "fa_sq", [128, 2, 2, 512], F32)
            nrm = sb(es, nc, "fa_nrm", [128, 2, 4, 8], F32)
            rot = sb(es, nc, "fa_rot", [128, 2, 4, 64], F32)
            qr2 = sb(es, nc, "fa_qr", [128, 2, 2, 512], F32)
            qro = sb(es, nc, "fa_qro", [128, 2, 2, 512], BF16)
            ssb8 = sb(es, nc, "fa_s", [128, 2, 8, 640], F32)
            pbf8 = sb(es, nc, "fa_p", [128, 2, 8, 640], BF16)
            pT5 = sb(es, nc, "fa_pT", [128, 2, 5, 128], BF16)
            rowv = sb(es, nc, "fa_row", [128, 2, 24], F32)
            oT = sb(es, nc, "fa_oT", [128, 4, 128], BF16)
            rsb = sb(es, nc, "fa_r", [128, 2, D], F32)
            osb = sb(es, nc, "fa_o", [128, 2, D], F32)
            p_q = pst(es, nc, "fa_p_q", [128, 2, 512], F32)
            p_tp = pst(es, nc, "fa_p_tp", [128, 2, 1024], BF16)
            p_s = pst(es, nc, "fa_p_s", [128, 2, 512], F32)
            p_oh = pst(es, nc, "fa_p_oh", [128, 2, 512], F32)
            rqT = p.regs_n("fa_qT", 2)
            rkT = p.regs_n("fa_kT", KR)
            rvB = p.regs_n("fa_vB", KR)
            roB = p.regs_n("fa_oB", 2)
            rht = p.regs_n("fa_h", 2)
            rhbf = p.regs_n("fa_hb", 2)
            rhT = p.regs_n("fa_hT", 2)
            rqf = p.regs_n("fa_qf", 2)
            rkf = p.regs_n("fa_kf", 2)
            rsq = p.regs_n("fa_sq", 4)
            rnrm = p.regs_n("fa_nrm", 4)
            rrot = p.regs_n("fa_rot", 2)
            rqr2 = p.regs_n("fa_qr", 4)
            rqro = p.regs_n("fa_qro", 2)
            rs8 = p.regs_n("fa_s", 2)
            rp8 = [p.regs_n(f"fa_p{i}", 8) for i in range(2)]
            rpT5 = p.regs_n("fa_pT", 2)
            rrow = p.regs_n("fa_row", 2)
            roT = p.reg("fa_oT")
            rrsb = p.regs_n("fa_r", 2)
            rosb = p.regs_n("fa_o", 2)
            rp_q = p.pregs_n("fa_p_q", 2)
            rp_tp = p.pregs_n("fa_p_tp", 2)
            rp_s = p.pregs_n("fa_p_s", 2)
            rp_oh = p.pregs_n("fa_p_oh", 2)

            p.emit("v", lambda v: v.memset(ssb8[...], 0.0),
                   writes=[rs8[0], rs8[1]], tag="fa_s0")

            def build(bi, tt):
                gtile = bi * NTB + tt
                sl = tt % 2
                kr = tt % KR
                _l2_shift_load(p, nc, ht[:, sl, :], rht[sl], rsrc_hbm,
                               gtile, bi, "fa_sh")
                p.emit("g", lambda v, sl=sl: v.tensor_copy(
                    hbf[:, sl, :], ht[:, sl, :]),
                    reads=[rht[sl]], writes=[rhbf[sl]], tag="fa_hb",
                    pool=True)
                def f_htp(t, sl=sl):
                    last = None
                    for j in range(KT):
                        last = t.transpose(p_tp[:, 0, ts(j)],
                                           hbf[:, sl, ts(j)], idb[:, :])
                    return last
                p.emit("t", f_htp, reads=[rhbf[sl], ridb],
                       writes=[rp_tp[0]], tag="fa_htp")
                p.emit("v", lambda v, sl=sl: v.tensor_copy(
                    hT[:, sl, :, :], p_tp[:, 0, :]),
                    reads=[rp_tp[0]], writes=[rhT[sl]], tag="fa_hte")
                for part in range(3):
                    s2 = part % 2
                    def f_qkv(t, part=part, s2=s2, sl=sl):
                        last = None
                        for k in range(KT):
                            last = t.matmul(
                                p_q[:, s2, :], hT[:, sl, k, :],
                                wqkv[:, k, ts(part, 512)],
                                start=(k == 0), stop=(k == KT - 1))
                        return last
                    p.emit("t", f_qkv, reads=[rhT[sl], rwqkv],
                           writes=[rp_q[s2]], tag="fa_qkv")
                    if part == 0:
                        p.emit("s", lambda s, s2=s2, sl=sl: s.copy(
                            qf[:, sl, :], p_q[:, s2, :]),
                            reads=[rp_q[s2]], writes=[rqf[sl]], tag="fa_qe")
                    elif part == 1:
                        p.emit("s", lambda s, s2=s2, sl=sl: s.copy(
                            kf[:, sl, :], p_q[:, s2, :]),
                            reads=[rp_q[s2]], writes=[rkf[sl]], tag="fa_ke")
                    else:
                        p.emit("v", lambda v, kr=kr, s2=s2: v.tensor_copy(
                            vB[:, kr, :], p_q[:, s2, :]),
                            reads=[rp_q[s2]], writes=[rvB[kr]], tag="fa_ve")
                for ci2, cn in enumerate(("c_l2_qm", "c_l2_qr",
                                          "c_l2_km", "c_l2_kr")):
                    p.emit("y", lambda y, ci2=ci2, cn=cn, tt=tt, sl=sl:
                           y.dma_start(out=rot[:, sl, ci2, :],
                                       in_=inp[cn][ts(tt), :]),
                           writes=[rrot[sl]], tag="fa_rld")
                for which in range(2):
                    sl2 = sl * 2 + which
                    xf = qf[:, sl, :] if which == 0 else kf[:, sl, :]
                    rxf = rqf[sl] if which == 0 else rkf[sl]
                    sc_t, rsc = (qs_t, rqs) if which == 0 else (ks_t, rks)
                    mi, ri_ = (0, 1) if which == 0 else (2, 3)
                    scale = (1.0 / 64.0) if which == 0 else 1.0
                    p.emit("s", lambda s, xf=xf, sl=sl, which=which:
                           s.activation(sqs_[:, sl, which, :], xf,
                                        AF.Square),
                           reads=[rxf], writes=[rsq[sl2]], tag="fa_n2")
                    p.emit("v", lambda v, sl=sl, which=which: v.reduce_sum(
                        nrm[:, sl, which * 2, :],
                        sqs_[:, sl, which, :].rearrange(
                            "p (h d) -> p h d", h=8), axis=AX.X),
                        reads=[rsq[sl2]], writes=[rnrm[sl2]], tag="fa_ns")
                    p.emit("s", lambda s, sl=sl, which=which, scale=scale:
                           s.activation(
                               nrm[:, sl, which * 2 + 1, :],
                               nrm[:, sl, which * 2, :], AF.Ln,
                               scale=scale),
                           reads=[rnrm[sl2]], writes=[rnrm[sl2]],
                           tag="fa_nq")
                    p.emit("s", lambda s, sl=sl, which=which:
                           s.activation(
                               nrm[:, sl, which * 2 + 1, :],
                               nrm[:, sl, which * 2 + 1, :], AF.Exp,
                               scale=-0.5),
                           reads=[rnrm[sl2]], writes=[rnrm[sl2]],
                           tag="fa_nq2")
                    p.emit("g", lambda v, xf=xf, sl=sl, which=which:
                           v.tensor_tensor(
                               xf.rearrange("p (h d) -> p h d", h=8),
                               xf.rearrange("p (h d) -> p h d", h=8),
                               nrm[:, sl, which * 2 + 1, :].rearrange(
                                   "p (h d) -> p h d", d=1).broadcast_to(
                                   [128, 8, 64]), ALU.mult),
                           reads=[rnrm[sl2], rxf], writes=[rxf], tag="fa_na",
                           pool=True)
                    p.emit("g", lambda v, xf=xf, sc_t=sc_t: v.tensor_tensor(
                        xf, xf, sc_t[:, :], ALU.mult),
                        reads=[rsc, rxf], writes=[rxf], tag="fa_nsc",
                        pool=True)
                    p.emit("g", lambda v, xf=xf, mi=mi, sl=sl, which=which:
                           v.tensor_tensor(
                               qr2[:, sl, which, :].rearrange(
                                   "p (h d) -> p h d", h=8),
                               xf.rearrange("p (h d) -> p h d", h=8),
                               rot[:, sl, mi, :].rearrange(
                                   "p (o d) -> p o d", o=1).broadcast_to(
                                   [128, 8, 64]),
                               ALU.mult),
                           reads=[rxf, rrot[sl]], writes=[rqr2[sl2]],
                           tag="fa_rm", pool=True)
                    for hf in range(2):
                        p.emit("g", lambda v, xf=xf, ri_=ri_, hf=hf, sl=sl,
                               which=which: v.tensor_tensor(
                                   sqs_[:, sl, which, 0:256].rearrange(
                                       "p (h d) -> p h d", h=8),
                                   xf.rearrange(
                                       "p (h q d) -> p h q d", h=8, q=2
                                   )[:, :, 1 - hf, :],
                                   rot[:, sl, ri_, hf * 32:(hf + 1) * 32
                                       ].rearrange("p (o d) -> p o d", o=1
                                                   ).broadcast_to(
                                       [128, 8, 32]),
                                   ALU.mult),
                               reads=[rxf, rrot[sl]], writes=[rsq[sl2]],
                               tag="fa_rt", pool=True)
                        p.emit("g", lambda v, which=which, hf=hf, sl=sl:
                               v.tensor_tensor(
                                   qro[:, sl, which, :].rearrange(
                                       "p (h q d) -> p h q d", h=8, q=2
                                   )[:, :, hf, :],
                                   qr2[:, sl, which, :].rearrange(
                                       "p (h q d) -> p h q d", h=8, q=2
                                   )[:, :, hf, :],
                                   sqs_[:, sl, which, 0:256].rearrange(
                                       "p (h d) -> p h d", h=8),
                                   ALU.add),
                               reads=[rqr2[sl * 2 + which], rsq[sl2]],
                               writes=[rqro[sl]], tag="fa_ra", pool=True)
            def qk_transpose(bi, tt):
                sl = tt % 2
                kr = tt % KR
                def f_qkt(t, sl=sl):
                    last = None
                    for which in range(2):
                        for j in range(4):
                            last = t.transpose(
                                p_tp[:, 0, which * 512 + j * 128:
                                     which * 512 + (j + 1) * 128],
                                qro[:, sl, which, ts(j)], idb[:, :])
                    return last
                p.emit("t", f_qkt, reads=[rqro[sl], ridb],
                       writes=[rp_tp[0]], tag="fa_qktp")
                p.emit("v", lambda v, dsl=tt % 2: v.tensor_copy(
                    qTh[:, :, dsl, :],
                    p_tp[:, 0, 0:512].rearrange("p (a b) -> p a b", a=4)),
                    reads=[rp_tp[0]], writes=[rqT[tt % 2]], tag="fa_qkte")
                p.emit("v", lambda v, kr=kr: v.tensor_copy(
                    kTh[:, :, kr, :],
                    p_tp[:, 0, 512:1024].rearrange("p (a b) -> p a b", a=4)),
                    reads=[rp_tp[0]], writes=[rkT[kr]], tag="fa_qkte")

            def attn_scores(bi, tt):
                kbase = max(tt - 4, 0)
                nk = tt - kbase + 1
                has_m0 = (tt >= 4)
                qsl = tt % 2
                for hp in range(4):
                    n4 = min(nk, 4)
                    def f_sim(t, tt=tt, hp=hp, kbase=kbase, n4=n4, qsl=qsl):
                        last = None
                        for s2h in range(2):
                            h8 = hp * 2 + s2h
                            pair, off = h8 // 2, 64 * (h8 % 2)
                            for j in range(n4):
                                last = t.matmul(
                                    p_s[:, s2h, j * 128:(j + 1) * 128],
                                    qTh[off:off + 64, pair, qsl, :],
                                    kTh[off:off + 64, pair,
                                        (kbase + j) % KR, :],
                                    start=True, stop=True)
                        return last
                    p.emit("t", f_sim,
                           reads=[rqT[qsl]] + [rkT[(kbase + j) % KR]
                                               for j in range(n4)],
                           writes=[rp_s[0], rp_s[1]], tag="fa_sim")
                    eng = "s" if (hp % 2 == 0 or L2BIS >= 1) else "v"
                    fn = (lambda e, hp=hp, n4=n4, qsl=qsl: e.copy(
                        ssb8[:, qsl, hp * 2:hp * 2 + 2, 0:n4 * 128],
                        p_s[:, 0:2, 0:n4 * 128])) if eng == "s" else (
                        lambda e, hp=hp, n4=n4, qsl=qsl: e.tensor_copy(
                            ssb8[:, qsl, hp * 2:hp * 2 + 2, 0:n4 * 128],
                            p_s[:, 0:2, 0:n4 * 128]))
                    p.emit(eng, fn, reads=[rp_s[0], rp_s[1]],
                           writes=[rs8[qsl]], tag="fa_sev")
                    if nk == 5:
                        def f_sim5(t, tt=tt, hp=hp, kbase=kbase, qsl=qsl):
                            last = None
                            for s2h in range(2):
                                h8 = hp * 2 + s2h
                                pair, off = h8 // 2, 64 * (h8 % 2)
                                last = t.matmul(
                                    p_s[:, s2h, 0:128],
                                    qTh[off:off + 64, pair, qsl, :],
                                    kTh[off:off + 64, pair,
                                        (kbase + 4) % KR, :],
                                    start=True, stop=True)
                            return last
                        p.emit("t", f_sim5,
                               reads=[rqT[qsl], rkT[(kbase + 4) % KR]],
                               writes=[rp_s[0], rp_s[1]], tag="fa_sim5")
                        p.emit(eng, (lambda e, hp=hp, qsl=qsl: e.copy(
                            ssb8[:, qsl, hp * 2:hp * 2 + 2, 512:640],
                            p_s[:, 0:2, 0:128])) if eng == "s" else (
                            lambda e, hp=hp, qsl=qsl: e.tensor_copy(
                                ssb8[:, qsl, hp * 2:hp * 2 + 2, 512:640],
                                p_s[:, 0:2, 0:128])),
                            reads=[rp_s[0], rp_s[1]], writes=[rs8[qsl]],
                            tag="fa_sev5")
                if has_m0:
                    p.emit("v", lambda v, qsl=qsl: v.tensor_tensor(
                        ssb8[:, qsl, :, 0:128], ssb8[:, qsl, :, 0:128],
                        m0[:, :].rearrange("p (o d) -> p o d", o=1
                                           ).broadcast_to([128, 8, 128]),
                        ALU.add),
                        reads=[rm0], writes=[rs8[qsl]], tag="fa_m0")
                p.emit("v", lambda v, nk=nk, qsl=qsl: v.tensor_tensor(
                    ssb8[:, qsl, :, (nk - 1) * 128:nk * 128],
                    ssb8[:, qsl, :, (nk - 1) * 128:nk * 128],
                    m4[:, :].rearrange("p (o d) -> p o d", o=1
                                       ).broadcast_to([128, 8, 128]),
                    ALU.add),
                    reads=[rm4], writes=[rs8[qsl]], tag="fa_m4")
                p.emit("v", lambda v, nk=nk, qsl=qsl: v.reduce_max(
                    rowv[:, qsl, 0:8], ssb8[:, qsl, :, 0:nk * 128],
                    axis=AX.X, negate=True),
                    reads=[rs8[qsl]], writes=[rrow[qsl]], tag="fa_max")
                subeng = "g"
                p.emit(subeng, lambda v, nk=nk, qsl=qsl: v.tensor_tensor(
                    ssb8[:, qsl, :, 0:nk * 128],
                    ssb8[:, qsl, :, 0:nk * 128],
                    rowv[:, qsl, 0:8].rearrange("p (h o) -> p h o", o=1
                                                ).broadcast_to(
                        [128, 8, nk * 128]), ALU.add),
                    reads=[rrow[qsl]], writes=[rs8[qsl]], tag="fa_sub",
                    pool=(subeng == "g"))
                p.emit("s", lambda s, qsl=qsl: s.activation(
                    pbf8[:, qsl, :, :].rearrange("p a b -> p (a b)"),
                    ssb8[:, qsl, :, :].rearrange("p a b -> p (a b)"),
                    AF.Exp),
                    reads=[rs8[qsl]], writes=rp8[qsl], tag="fa_expf")

            def attn_tail(bi, tt):
                kbase = max(tt - 4, 0)
                nk = tt - kbase + 1
                qsl = tt % 2
                for h8 in range(8):
                    s2h = (h8 % 2) if L2BIS < 2 else 0
                    g = h8 // 4
                    def f_ptp(t, nk=nk, h8=h8, s2h=s2h, qsl=qsl):
                        last = None
                        for j in range(nk):
                            last = t.transpose(
                                p_tp[:, s2h, ts(j)],
                                pbf8[:, qsl, h8, ts(j)], idb[:, :])
                        return last
                    p.emit("t", f_ptp, reads=[rp8[qsl][h8], ridb],
                           writes=[rp_tp[s2h]], tag="fa_ptp")
                    if h8 % 2 == 0:
                        p.emit("s", lambda s, nk=nk, s2h=s2h: s.copy(
                            pT5[:, s2h, 0:nk, :].rearrange(
                                "p a b -> p (a b)"),
                            p_tp[:, s2h, 0:nk * 128]),
                            reads=[rp_tp[s2h]], writes=[rpT5[s2h]],
                            tag="fa_pte")
                    else:
                        p.emit("v", lambda v, nk=nk, s2h=s2h: v.tensor_copy(
                            pT5[:, s2h, 0:nk, :].rearrange(
                                "p a b -> p (a b)"),
                            p_tp[:, s2h, 0:nk * 128]),
                            reads=[rp_tp[s2h]], writes=[rpT5[s2h]],
                            tag="fa_pte")
                    def f_oh(t, h8=h8, g=g, kbase=kbase, nk=nk, s2h=s2h):
                        c0 = (h8 % 4) * 64
                        last = None
                        for j in range(nk):
                            last = t.matmul(
                                p_oh[:, g, c0:c0 + 64],
                                pT5[:, s2h, j, :],
                                vB[:, (kbase + j) % KR,
                                   h8 * 64:(h8 + 1) * 64],
                                start=(j == 0), stop=(j == nk - 1))
                        if L2BIS >= 3:
                            return last
                        for j in range(nk):
                            last = t.matmul(
                                p_oh[:, g, 256 + (h8 % 4):257 + (h8 % 4)],
                                pT5[:, s2h, j, :], onesc[:, :],
                                start=(j == 0), stop=(j == nk - 1))
                        return last
                    p.emit("t", f_oh,
                           reads=[rpT5[s2h], ronesc]
                           + [rvB[(kbase + j) % KR] for j in range(nk)],
                           writes=[rp_oh[g]], tag="fa_ohm")
                    if h8 % 4 == 3:
                        if L2BIS >= 3:
                            p.emit("v", lambda v, tt=tt, g=g:
                                   v.tensor_copy(
                                       oB[:, tt % 2, g * 256:(g + 1) * 256],
                                       p_oh[:, g, 0:256]),
                                   reads=[rp_oh[g]], writes=[roB[tt % 2]],
                                   tag="fa_ohe0")
                            continue
                        p.emit("v", lambda v, g=g, qsl=qsl: v.tensor_copy(
                            rowv[:, qsl, 8 + g * 4:12 + g * 4],
                            p_oh[:, g, 256:260]),
                            reads=[rp_oh[g]], writes=[rrow[qsl]],
                            tag="fa_sce")
                        p.emit("v", lambda v, g=g, qsl=qsl: v.reciprocal(
                            rowv[:, qsl, 16 + g * 4:20 + g * 4],
                            rowv[:, qsl, 8 + g * 4:12 + g * 4]),
                            reads=[rrow[qsl]], writes=[rrow[qsl]],
                            tag="fa_rec")
                        p.emit("v", lambda v, tt=tt, g=g, qsl=qsl:
                               v.tensor_tensor(
                                   oB[:, tt % 2, g * 256:(g + 1) * 256
                                      ].rearrange("p (h d) -> p h d", h=4),
                                   p_oh[:, g, 0:256].rearrange(
                                       "p (h d) -> p h d", h=4),
                                   rowv[:, qsl, 16 + g * 4:20 + g * 4
                                        ].rearrange("p (h o) -> p h o", o=1
                                                    ).broadcast_to(
                                       [128, 4, 64]), ALU.mult),
                               reads=[rp_oh[g], rrow[qsl]],
                               writes=[roB[tt % 2]], tag="fa_ohe")

            def wo_tile(bi, tt):
                gtile = bi * NTB + tt
                sl = tt % 2
                def f_otp(t, sl=sl):
                    last = None
                    for j in range(4):
                        last = t.transpose(p_tp[:, 0, ts(j)],
                                           oB[:, sl, ts(j)], idb[:, :])
                    return last
                p.emit("t", f_otp, reads=[roB[sl], ridb],
                       writes=[rp_tp[0]], tag="fa_otp")
                p.emit("v", lambda v: v.tensor_copy(
                    oT[:, :, :].rearrange("p a b -> p (a b)"),
                    p_tp[:, 0, 0:512]),
                    reads=[rp_tp[0]], writes=[roT], tag="fa_ote")
                p.emit("y", lambda y, gtile=gtile, sl=sl: y.dma_start(
                    out=rsb[:, sl, :], in_=rsrc_hbm[ts(gtile), :]),
                    writes=[rrsb[sl]], tag="fa_rld2")
                for half in range(2):
                    def f_wo(t, half=half):
                        last = None
                        for k in range(4):
                            last = t.matmul(p_q[:, half, :], oT[:, k, :],
                                            wo_[:, k, ts(half, 512)],
                                            start=(k == 0), stop=(k == 3))
                        return last
                    p.emit("t", f_wo, reads=[roT, rwo],
                           writes=[rp_q[half]], tag="fa_wom")
                    p.emit("v", lambda v, half=half, sl=sl: v.tensor_add(
                        osb[:, sl, ts(half, 512)], p_q[:, half, :],
                        rsb[:, sl, ts(half, 512)]),
                        reads=[rp_q[half], rrsb[sl]], writes=[rosb[sl]],
                        tag="fa_radd")
                p.emit("g", lambda g, gtile=gtile, sl=sl: g.dma_start(
                    out=rtgt_hbm[ts(gtile), :], in_=osb[:, sl, :]),
                    reads=[rosb[sl]], tag="fa_rst")

            for bi in range(B_PER_CORE):
                for tt in range(NTB):
                    if tt >= 2:
                        wo_tile(bi, tt - 2)
                    build(bi, tt)
                    if tt >= 1:
                        attn_tail(bi, tt - 1)
                    qk_transpose(bi, tt)
                    attn_scores(bi, tt)
                attn_tail(bi, NTB - 1)
                wo_tile(bi, NTB - 2)
                wo_tile(bi, NTB - 1)
    p.barrier(tag="l2fa_end")




def phase_l2_ffn_mlp(p, rsrc_hbm, rtgt_hbm, inp, li, ep=None):
    nc = p.nc
    HIDP = 2816
    NKH = HIDP // 128
    with contextlib.ExitStack() as st:
        w1a, rw1a = wload(p, st, f"l2w1a{li}", inp["L2c_w1a"][li], D, HIDP)
        w1g, rw1g = wload(p, st, f"l2w1g{li}", inp["L2c_w1g"][li], D, HIDP)
        w2_, rw2 = wload(p, st, f"l2w2_{li}", inp["L2c_w2"][li], HIDP, D)
        b1c, rb1 = cload(p, st, f"l2b1{li}", inp["L2c_b1"][li], [128, 2, NKH])
        b2r, rb2 = cload(p, st, f"l2b2{li}", inp["L2c_b2"][li], [1, D], BF16)
        lng, rlng = cload(p, st, f"l2lng{li}", inp["L2c_lng"][li], [128, D],
                          BF16)
        lnb, rlnb = cload(p, st, f"l2lnb{li}", inp["L2c_lnb"][li], [128, D],
                          BF16)
        ones1, ro1 = cload(p, st, f"l2fo1{li}", inp["c_ones1_bf"], [1, 128],
                           BF16)
        idb, ridb = cload(p, st, f"l2gid{li}", inp["c_ident_bf"], [128, 128],
                          BF16)
        epsl, reps = eps_tile(p, st, f"l2eps{li}", 1e-5)
        with contextlib.ExitStack() as es:
            ht = sb(es, nc, "fm_h", [128, 2, D], F32)
            sts = sb(es, nc, "fm_st", [128, 2, 8], F32)
            gsc = sb(es, nc, "fm_gs", [128, 512], F32)
            xb = sb(es, nc, "fm_xb", [128, 2, D], BF16)
            xT = sb(es, nc, "fm_xT", [128, 2, KT, 512], BF16)
            aT = sb(es, nc, "fm_aT", [128, 512], BF16)
            hT2 = sb(es, nc, "fm_hT", [128, NKH, 512], BF16)
            rsb = sb(es, nc, "fm_r", [128, 2, D], F32)
            osb = sb(es, nc, "fm_o", [128, 2, D], F32)
            p_a = pst(es, nc, "fm_p_a", [128, 2, 512], F32)
            p_tp = pst(es, nc, "fm_p_tp", [128, 2, 1024], BF16)
            p_o = pst(es, nc, "fm_p_o", [128, 2, 1024], F32)
            rht = p.regs_n("fm_h", 2)
            rsts = p.regs_n("fm_sts", 2)
            rgsc = p.reg("fm_gs")
            rxb = p.regs_n("fm_xb", 2)
            rxT = p.regs_n("fm_xT", 2)
            raT = p.reg("fm_aT")
            rhT2 = p.regs_n("fm_hT", 2)
            rrsb = p.regs_n("fm_r", 2)
            rosb = p.regs_n("fm_o", 2)
            rp_a = p.pregs_n("fm_p_a", 2)
            rp_tp = p.pregs_n("fm_p_tp", 2)
            rp_o = p.pregs_n("fm_p_o", 2)
            for bi in range(B_PER_CORE):
                for cc in range(S // 512):
                    for sub in range(4):
                        gtile = bi * (S // 128) + cc * 4 + sub
                        sl = sub % 2
                        _l2_shift_load(p, nc, ht[:, sl, :], rht[sl], rsrc_hbm,
                                       gtile, bi, "fm_sh")
                        p.emit("s", lambda s, sl=sl: s.activation(
                            xb[:, sl, :], ht[:, sl, :], AF.Square,
                            accum_out=sts[:, sl, 1:2]),
                            reads=[rht[sl]], writes=[rsts[sl], rxb[sl]],
                            tag="fm_sq")
                        p.emit("s", lambda s, sl=sl: s.activation(
                            xb[:, sl, :], ht[:, sl, :], AF.Identity,
                            accum_out=sts[:, sl, 0:1]),
                            reads=[rht[sl]], writes=[rsts[sl], rxb[sl]],
                            tag="fm_sum")
                        p.emit("s", lambda s, sl=sl: s.activation(
                            sts[:, sl, 2:3], sts[:, sl, 0:1], AF.Identity,
                            scale=-1.0 / D),
                            reads=[rsts[sl]], writes=[rsts[sl]], tag="fm_nm")
                        p.emit("s", lambda s, sl=sl: s.activation(
                            sts[:, sl, 3:4], sts[:, sl, 2:3], AF.Square),
                            reads=[rsts[sl]], writes=[rsts[sl]], tag="fm_m2")
                        p.emit("s", lambda s, sl=sl: s.activation(
                            sts[:, sl, 4:5], sts[:, sl, 3:4], AF.Identity,
                            scale=-1.0, bias=epsl[:, :]),
                            reads=[rsts[sl], reps], writes=[rsts[sl]],
                            tag="fm_em2")
                        p.emit("s", lambda s, sl=sl: s.activation(
                            sts[:, sl, 5:6], sts[:, sl, 1:2], AF.Sqrt,
                            scale=1.0 / D, bias=sts[:, sl, 4:5]),
                            reads=[rsts[sl]], writes=[rsts[sl]],
                            tag="fm_std")
                        p.emit("v", lambda v, sl=sl: v.reciprocal(
                            sts[:, sl, 6:7], sts[:, sl, 5:6]),
                            reads=[rsts[sl]], writes=[rsts[sl]], tag="fm_rst")
                        p.emit("v", lambda v, sl=sl: v.tensor_scalar(
                            ht[:, sl, :], ht[:, sl, :], sts[:, sl, 2:3],
                            sts[:, sl, 6:7], ALU.add, ALU.mult),
                            reads=[rht[sl], rsts[sl]], writes=[rht[sl]],
                            tag="fm_ln")
                        p.emit("v", lambda v, sl=sl: v.tensor_mul(
                            ht[:, sl, :], ht[:, sl, :], lng[:, :]),
                            reads=[rht[sl], rlng], writes=[rht[sl]],
                            tag="fm_lng")
                        p.emit("v", lambda v, sl=sl: v.tensor_tensor(
                            xb[:, sl, :], ht[:, sl, :], lnb[:, :], ALU.add),
                            reads=[rht[sl], rlnb], writes=[rxb[sl]],
                            tag="fm_lnb")
                        def f_xtp(t, sl=sl):
                            last = None
                            for j in range(KT):
                                last = t.transpose(p_tp[:, sl, ts(j)],
                                                   xb[:, sl, ts(j)], idb[:, :])
                            return last
                        p.emit("t", f_xtp, reads=[rxb[sl], ridb],
                               writes=[rp_tp[sl]], tag="fm_xtp")
                        p.emit("v", lambda v, sl=sl, sub=sub, cs=cc % 2:
                               v.tensor_copy(
                                   xT[:, cs, :, sub * 128:(sub + 1) * 128],
                                   p_tp[:, sl, :].rearrange(
                                       "p (a b) -> p a b", a=KT)),
                               reads=[rp_tp[sl]], writes=[rxT[cc % 2]],
                               tag="fm_xte")
                    def hidden(n):
                        ns = n
                        ng = 0 if n < 11 else 1
                        def f_a(t, n=n, cs=cc % 2):
                            last = None
                            for k in range(KT):
                                last = t.matmul(p_a[:, 0, :], w1a[:, k, ts(n)],
                                                xT[:, cs, k, :],
                                                start=(k == 0), stop=(k == KT - 1))
                            return last
                        p.emit("t", f_a, reads=[rxT[cc % 2], rw1a],
                               writes=[rp_a[0]], tag="fm_am")
                        p.emit("s", lambda s, n=n: s.activation(
                            aT[:, :], p_a[:, 0, :], AF.Identity,
                            bias=b1c[:, 0, n:n + 1]),
                            reads=[rp_a[0], rb1], writes=[raT], tag="fm_ae")
                        def f_g(t, n=n, cs=cc % 2):
                            last = None
                            for k in range(KT):
                                last = t.matmul(p_a[:, 1, :], w1g[:, k, ts(n)],
                                                xT[:, cs, k, :],
                                                start=(k == 0), stop=(k == KT - 1))
                            return last
                        p.emit("t", f_g, reads=[rxT[cc % 2], rw1g],
                               writes=[rp_a[1]], tag="fm_gm")
                        p.emit("s", lambda s, n=n: s.activation(
                            gsc[:, :], p_a[:, 1, :], AF.Gelu,
                            bias=b1c[:, 1, n:n + 1]),
                            reads=[rp_a[1], rb1], writes=[rgsc], tag="fm_ge")
                        p.emit("v", lambda v, ns=ns: v.tensor_tensor(
                            hT2[:, ns, :], aT[:, :], gsc[:, :], ALU.mult),
                            reads=[raT, rgsc], writes=[rhT2[ng]], tag="fm_hm")

                    for n in range(NKH):
                        hidden(n)
                    for sub in range(4):
                        sl = sub % 2
                        def f_w2a(t, sub=sub, sl=sl):
                            last = None
                            for half in range(2):
                                last = t.matmul(p_o[:, sl, ts(half, 512)],
                                                ones1[:, :],
                                                b2r[0:1, ts(half, 512)],
                                                start=True, stop=False)
                                for k in range(11):
                                    last = t.matmul(
                                        p_o[:, sl, ts(half, 512)],
                                        hT2[:, k, sub * 128:(sub + 1) * 128],
                                        w2_[:, k, ts(half, 512)],
                                        start=False, stop=False)
                            return last
                        p.emit("t", f_w2a, reads=[rhT2[0], rw2, rb2, ro1],
                               writes=[rp_o[sl]], tag="fm_w2a")
                        gtile = bi * (S // 128) + cc * 4 + sub
                        def f_w2b(t, sub=sub, sl=sl):
                            last = None
                            for half in range(2):
                                for k in range(11, NKH):
                                    last = t.matmul(
                                        p_o[:, sl, ts(half, 512)],
                                        hT2[:, k,
                                            sub * 128:(sub + 1) * 128],
                                        w2_[:, k, ts(half, 512)],
                                        start=False, stop=(k == NKH - 1))
                            return last
                        p.emit("t", f_w2b, reads=[rhT2[1], rw2],
                               writes=[rp_o[sl]], tag="fm_w2b")
                        p.emit("y", lambda y, gtile=gtile, sl=sl: y.dma_start(
                            out=rsb[:, sl, :], in_=rsrc_hbm[ts(gtile), :]),
                            writes=[rrsb[sl]], tag="fm_rld")
                        p.emit("v", lambda v, sl=sl: v.tensor_add(
                            osb[:, sl, :], p_o[:, sl, :], rsb[:, sl, :]),
                            reads=[rp_o[sl], rrsb[sl]], writes=[rosb[sl]],
                            tag="fm_radd")
                        if ep is not None:
                            p.emit("y", lambda y, gtile=gtile, sl=sl:
                                   y.dma_start(out=ht[:, sl, :],
                                               in_=ep[0][ts(gtile), :]),
                                   writes=[rht[sl]], tag="fm_exl")
                            p.emit("v", lambda v, sl=sl: v.tensor_add(
                                osb[:, sl, :], osb[:, sl, :], ht[:, sl, :]),
                                reads=[rosb[sl], rht[sl]], writes=[rosb[sl]],
                                tag="fm_ea1")
                            p.emit("y", lambda y, gtile=gtile, sl=sl:
                                   y.dma_start(out=ht[:, sl, :],
                                               in_=ep[1][ts(gtile), :]),
                                   writes=[rht[sl]], tag="fm_eal")
                            p.emit("v", lambda v, sl=sl: v.tensor_add(
                                osb[:, sl, :], osb[:, sl, :], ht[:, sl, :]),
                                reads=[rosb[sl], rht[sl]], writes=[rosb[sl]],
                                tag="fm_ea2")
                        p.emit("g", lambda g, gtile=gtile, sl=sl: g.dma_start(
                            out=rtgt_hbm[ts(gtile), :], in_=osb[:, sl, :]),
                            reads=[rosb[sl]], tag="fm_rst")
    p.barrier(tag="l2fm_end")


def phase_l2_epilogue(p, x_hbm, attn_hbm, ffn_hbm, xout_hbm):
    nc = p.nc
    with contextlib.ExitStack() as es:
        xsb = sb(es, nc, "e2_x", [128, 2, D], F32)
        atb = sb(es, nc, "e2_at", [128, 2, D], F32)
        ffb = sb(es, nc, "e2_ff", [128, 2, D], F32)
        outb = sb(es, nc, "e2_o", [128, 2, D], F32)
        rx = p.regs_n("e2_x", 2)
        rat = p.regs_n("e2_at", 2)
        rff = p.regs_n("e2_ff", 2)
        rob = p.regs_n("e2_o", 2)
        for tt in range(NTILE):
            sl = tt % 2
            p.emit("y", lambda y, tt=tt, sl=sl: y.dma_start(
                out=xsb[:, sl, :], in_=x_hbm[ts(tt), :]),
                writes=[rx[sl]], tag="e2_xl")
            p.emit("y", lambda y, tt=tt, sl=sl: y.dma_start(
                out=atb[:, sl, :], in_=attn_hbm[ts(tt), :]),
                writes=[rat[sl]], tag="e2_al")
            p.emit("y", lambda y, tt=tt, sl=sl: y.dma_start(
                out=ffb[:, sl, :], in_=ffn_hbm[ts(tt), :]),
                writes=[rff[sl]], tag="e2_fl")
            p.emit("v", lambda v, sl=sl: v.tensor_add(
                outb[:, sl, :], xsb[:, sl, :], atb[:, sl, :]),
                reads=[rx[sl], rat[sl]], writes=[rob[sl]], tag="e2_a1")
            p.emit("v", lambda v, sl=sl: v.tensor_add(
                outb[:, sl, :], outb[:, sl, :], ffb[:, sl, :]),
                reads=[rob[sl], rff[sl]], writes=[rob[sl]], tag="e2_a2")
            p.emit("g", lambda g, tt=tt, sl=sl: g.dma_start(
                out=xout_hbm[ts(tt), :], in_=outb[:, sl, :]),
                reads=[rob[sl]], tag="e2_st")
    p.barrier(tag="e2_end")


def l2_host_relayout(inputs, core_id):
    d = {}
    f = lambda a: np.ascontiguousarray(a, dtype=np.float32)
    d["L2_attn_Wq"] = f(inputs["L2_attn_Wq"])
    d["L2c_bq"] = f(inputs["L2_attn_bq"]).reshape(KT, 128).T.copy()
    d["L2_ffn_wqkv"] = f(inputs["L2_ffn_wqkv"])
    d["L2_ffn_wo"] = f(inputs["L2_ffn_wo"])
    qs = f(inputs["L2_ffn_q_scale"])
    ks = f(inputs["L2_ffn_k_scale"])
    d["L2c_qs"] = np.broadcast_to(np.tile(qs, (1, 8))[:, None, :],
                                  (2, 128, 512)).copy()
    d["L2c_ks"] = np.broadcast_to(np.tile(ks, (1, 8))[:, None, :],
                                  (2, 128, 512)).copy()
    w1 = f(inputs["L2_ffn_w1"])
    HID = 2730
    pad = np.zeros((2, D, 2816 - HID), np.float32)
    d["L2c_w1a"] = np.concatenate([w1[:, :, 0:HID], pad], axis=2).copy()
    d["L2c_w1g"] = np.concatenate([w1[:, :, HID:], pad], axis=2).copy()
    b1 = f(inputs["L2_ffn_b1"])
    b1p = np.zeros((2, 2, 2816), np.float32)
    b1p[:, 0, 0:HID] = b1[:, 0:HID]
    b1p[:, 1, 0:HID] = b1[:, HID:]
    d["L2c_b1"] = b1p.reshape(2, 2, 22, 128).transpose(0, 3, 1, 2).copy()
    w2 = f(inputs["L2_ffn_w2"])
    pad2 = np.zeros((2, 2816 - HID, D), np.float32)
    d["L2c_w2"] = np.concatenate([w2, pad2], axis=1).copy()
    d["L2c_b2"] = f(inputs["L2_ffn_b2"]).reshape(2, 1, D)
    d["L2c_lng"] = np.broadcast_to(f(inputs["L2_ffn_ln_g"])[:, None, :],
                                   (2, 128, D)).copy()
    d["L2c_lnb"] = np.broadcast_to(f(inputs["L2_ffn_ln_b"])[:, None, :],
                                   (2, 128, D)).copy()
    return d


def l2_rotary_consts():
    DIM_HEAD = 64
    WINDOW = 512
    n = S
    half = np.arange(0, DIM_HEAD, 2, dtype=np.float64)
    inv_freq = 1.0 / 10000.0 ** (half / DIM_HEAD)
    t = np.arange(n, dtype=np.float64)
    freqs = t[:, None] * inv_freq[None, :]
    freqs = np.concatenate([freqs, freqs], axis=-1)
    base = (half + 0.4 * DIM_HEAD) / (1.4 * DIM_HEAD)
    power = (t - n // 2) / float(WINDOW // 2)
    scale = base[None, :] ** power[:, None]
    scale = np.concatenate([scale, scale], axis=-1)
    cos, sin = np.cos(freqs), np.sin(freqs)
    sign = np.where(np.arange(DIM_HEAD) < 32, -1.0, 1.0)[None, :]
    c = {}
    c["c_l2_qm"] = (cos * scale).astype(np.float32)
    c["c_l2_qr"] = (sin * scale * sign).astype(np.float32)
    c["c_l2_km"] = (cos / scale).astype(np.float32)
    c["c_l2_kr"] = (sin / scale * sign).astype(np.float32)
    i = np.arange(128)
    m0 = np.where(i[None, :] >= i[:, None] + 1, 0.0, -1e30)
    m4 = np.where(i[None, :] <= i[:, None], 0.0, -1e30)
    c["c_l2_m0"] = m0.astype(np.float32)
    c["c_l2_m4"] = m4.astype(np.float32)
    return c


def build_layer2(consts, debug=False):
    nc = bass.Bass("TRN2", target_bir_lowering=False)
    x_in = nc.dram_tensor("x_in", [T, D], F32, kind="ExternalInput").ap()
    x_out = nc.dram_tensor("x_out", [T, D], F32, kind="ExternalOutput").ap()
    kind = "ExternalOutput" if debug else "Internal"
    xn = nc.dram_tensor("dbg_xn", [T, D], F32, kind=kind).ap()
    xnT = nc.dram_tensor("dbg_xnT", [D, T], BF16, kind=kind).ap()
    attn = nc.dram_tensor("dbg_attn", [T, D], F32, kind=kind).ap()
    ra = nc.dram_tensor("l2_ra", [T, D], F32, kind="Internal").ap()
    rb = nc.dram_tensor("l2_rb", [T, D], F32,
                        kind="ExternalOutput" if debug else "Internal").ap()
    shapes = {
        "L2_attn_Wq": ([D, D], F32), "L2c_bq": ([128, KT], F32),
        "L2_ffn_wqkv": ([2, D, 1536], F32), "L2_ffn_wo": ([2, 512, D], F32),
        "L2c_qs": ([2, 128, 512], F32), "L2c_ks": ([2, 128, 512], F32),
        "L2c_w1a": ([2, D, 2816], F32), "L2c_w1g": ([2, D, 2816], F32),
        "L2c_b1": ([2, 128, 2, 22], F32), "L2c_w2": ([2, 2816, D], F32),
        "L2c_b2": ([2, 1, D], F32), "L2c_lng": ([2, 128, D], F32),
        "L2c_lnb": ([2, 128, D], F32),
    }
    inp = {}
    for n, (shp, dt) in shapes.items():
        inp[n] = nc.dram_tensor(n, shp, dt, kind="ExternalInput").ap()
    for cn, arr in consts.items():
        dt = BF16 if arr.dtype == BF else (I32 if arr.dtype == np.int32 else F32)
        inp[cn] = nc.dram_tensor(cn, list(arr.shape), dt,
                                 kind="ExternalInput").ap()
    p = Prog(nc)
    with contextlib.ExitStack() as st0:
        phase_ln(p, st0, x_in, xn, xnT, inp)
    phase_l2_attn(p, xnT, attn, inp)
    phase_l2_ffn_attn(p, xn, ra, inp, 0)
    phase_l2_ffn_mlp(p, ra, rb, inp, 0)
    phase_l2_ffn_attn(p, rb, ra, inp, 1)
    phase_l2_ffn_mlp(p, ra, rb, inp, 1)
    phase_l2_epilogue(p, x_in, attn, rb, x_out)
    p.build()
    return nc, p



def phase_l3_attn(p, xnT_hbm, attn_hbm, inp):
    import os
    CUT = int(os.environ.get("L3CUT", "99"))
    nc = p.nc
    CHT = 256
    C2 = 64
    NOUT = S // CHT
    NIN = CHT // C2
    H16 = 16
    with contextlib.ExitStack() as st:
        wr_, rwr = wload(p, st, "l3wr", inp["L3_attn_Wr"], D, D)
        wk_, rwk = wload(p, st, "l3wk", inp["L3_attn_Wk"], D, D)
        wv_, rwv = wload(p, st, "l3wv", inp["L3_attn_Wv"], D, D)
        wo_, rwo = wload(p, st, "l3wo", inp["L3_attn_Wo"], D, D)
        mw1, rmw1 = wload(p, st, "l3mw1", inp["L3_attn_maa_w1"], D, 160)
        mw2, rmw2 = cload(p, st, "l3mw2", inp["L3c_maa_w2"], [32, 5, D], BF16)
        dw1, rdw1 = wload(p, st, "l3dw1", inp["L3_attn_decay_w1"], D, 64)
        dw2, rdw2 = cload(p, st, "l3dw2", inp["L3c_decay_w2"], [64, D], BF16)
        gw1, rgw1 = wload(p, st, "l3gw1", inp["L3_attn_gate_w1"], D, 64)
        gw2, rgw2 = cload(p, st, "l3gw2", inp["L3c_gate_w2"], [64, D], BF16)
        maas, rmaas = cload(p, st, "l3maa", inp["L3c_maa"], [128, 6, KT])
        tdc, rtdc = cload(p, st, "l3td", inp["L3c_td"], [128, KT])
        u_c, ru = cload(p, st, "l3u", inp["L3c_u"], [128, KT])
        lng_t, rlng = cload(p, st, "l3lng", inp["L3c_lng"], [128, D])
        lnb_t, rlnb = cload(p, st, "l3lnb", inp["L3c_lnb"], [128, D])
        m12c, rm12 = cload(p, st, "l3m12", inp["c_l3_m12"], [64, 64])
        onesc, ronesc = cload(p, st, "l3oc", inp["c_ones128_bf"], [128, 1],
                              BF16)
        idb, ridb = cload(p, st, "l3id", inp["c_ident_bf"], [128, 128], BF16)
        eps3, reps3 = eps_tile(p, st, "l3eps", 0.00064)
        with contextlib.ExitStack() as es:
            xnT = sb(es, nc, "a3_xnT", [128, 2, KT, CHT], BF16)
            xx = sb(es, nc, "a3_xx", [128, KT, CHT], F32)
            lora5 = sb(es, nc, "a3_lora", [32, 5, CHT], BF16)
            tf = sb(es, nc, "a3_tf", [128, KT, CHT], F32)
            mixs = sb(es, nc, "a3_mix", [128, 2, KT, CHT], BF16)
            rT = sb(es, nc, "a3_rT", [128, KT, CHT], BF16)
            kT = sb(es, nc, "a3_kT", [128, KT, CHT], BF16)
            vB = sb(es, nc, "a3_vB", [128, 2, D], BF16)
            gB = sb(es, nc, "a3_gB", [128, 2, D], BF16)
            t64 = sb(es, nc, "a3_t64", [64, CHT], BF16)
            cs = sb(es, nc, "a3_cs", [128, KT, CHT + 1], F32)
            csc = sb(es, nc, "a3_csc", [128, KT, 1], F32)
            bsc = sb(es, nc, "a3_bsc", [128, KT, 4], F32)
            ex = sb(es, nc, "a3_ex", [128, 2, KT, C2], F32)
            rta = sb(es, nc, "a3_rta", [128, 2, KT, C2], BF16)
            rtb = sb(es, nc, "a3_rtb", [128, 2, KT, C2], BF16)
            kta = sb(es, nc, "a3_kta", [128, 2, KT, C2], BF16)
            ktb = sb(es, nc, "a3_ktb", [128, 2, KT, C2], BF16)
            rhat = sb(es, nc, "a3_rh", [128, 2, KT, C2], BF16)
            k2t = sb(es, nc, "a3_k2", [128, 2, KT, C2], BF16)
            e2t = sb(es, nc, "a3_e2", [128, 2, KT, C2], BF16)
            pce = sb(es, nc, "a3_pc", [128, KT, 1], F32)
            zcht = sb(es, nc, "a3_zz", [128, CHT], F32)
            at_ = sb(es, nc, "a3_at", [64, 4, 64], BF16)
            vst = sb(es, nc, "a3_vst", [64, 2, D], BF16)
            k2T = sb(es, nc, "a3_k2T", [64, 4, 64], BF16)
            Zf = sb(es, nc, "a3_Zf", [64, 16, 64], F32)
            Zb = sb(es, nc, "a3_Zb", [64, 16, 64], BF16)
            rh2 = sb(es, nc, "a3_rh2", [64, 2, 16, 64], BF16)
            pc2 = sb(es, nc, "a3_pc2", [64, 2, 16], F32)
            dsb = sb(es, nc, "a3_d", [128, 2, 16], F32)
            Yc = sb(es, nc, "a3_Y", [128, 2, D], F32)
            Yd = sb(es, nc, "a3_Yd", [128, D], F32)
            gst = sb(es, nc, "a3_gst", [128, 6, 16], F32)
            gt = sb(es, nc, "a3_gt", [128, 2, D], BF16)
            gtT = sb(es, nc, "a3_gtT", [128, KT, 128], BF16)
            osb = sb(es, nc, "a3_o", [128, 1, D], F32)
            p_m = pst(es, nc, "a3_p_m", [128, 2, 512], F32)
            p_a = pst(es, nc, "a3_p_a", [64, 4, 512], F32)
            p_z = pst(es, nc, "a3_p_z", [64, 512], F32)
            p_tp = pst(es, nc, "a3_p_tp", [128, 1024], BF16)
            rxn = p.regs_n("a3_xnT", 2)
            rxx = p.reg("a3_xx")
            rlora, rtf = p.reg("a3_lora"), p.reg("a3_tf")
            rmix = p.regs_n("a3_mix", 2)
            xxx = mixs[:, 1]
            rxxx = rmix[1]
            rrT, rkT, rvB, rgB = (p.reg("a3_rT"), p.reg("a3_kT"),
                                  p.reg("a3_vB"), p.reg("a3_gB"))
            ew = tf
            rt64, rcs, rcsc = (p.reg("a3_t64"), p.reg("a3_cs"),
                               p.reg("a3_csc"))
            rew = rtf
            rbsc = p.reg("a3_bsc")
            rex = p.regs_n("a3_ex", 2)
            rrta = p.regs_n("a3_rta", 2)
            rrtb = p.regs_n("a3_rtb", 2)
            rkta = p.regs_n("a3_kta", 2)
            rktb = p.regs_n("a3_ktb", 2)
            rrhat = p.regs_n("a3_rh", 2)
            rk2 = p.regs_n("a3_k2", 2)
            re2 = p.regs_n("a3_e2", 2)
            rpce = p.reg("a3_pc")
            rat_ = p.regs_n("a3_at", 4)
            rk2T = p.regs_n("a3_k2T", 4)
            rvst = p.regs_n("a3_vst", 2)
            rzz = p.reg("a3_zz")
            rZf = p.regs_n("a3_Zf", 16)
            rZb = p.regs_n("a3_Zb", 16)
            rrh2 = p.regs_n("a3_rh2", 2)
            rpc2 = p.regs_n("a3_pc2", 2)
            rd = p.regs_n("a3_d", 2)
            rY = p.regs_n("a3_Y", 2)
            rYd, rgst = p.reg("a3_Yd"), p.reg("a3_gst")
            rgt, rgtT = p.reg("a3_gt"), p.reg("a3_gtT")
            rosb = p.regs_n("a3_o", 1) * 2
            rp_m = p.pregs_n("a3_p_m", 2)
            rp_a = p.pregs_n("a3_p_a", 4)
            rp_z = p.preg("a3_p_z")
            rp_tp = p.preg("a3_p_tp")
            xnTr = xnT_hbm.rearrange("(kt r) t -> r kt t", r=128)
            p.emit("v", lambda v: v.memset(zcht[...], 0.0), writes=[rzz],
                   tag="a3_zz0")
            rzz.const = True
            for s0 in range(4):
                p.emit("v", lambda v, s0=s0: v.memset(
                    p_a[32:64, s0, 0:32], 0.0), writes=[rp_a[s0]],
                    tag="a3_pz0")


            def bcKT(apcol, n):
                return apcol.broadcast_to([128, KT, n])

            for bi in range(B_PER_CORE):
                p.emit("v", lambda v: v.memset(Zf[...], 0.0), writes=rZf,
                       tag="a3_z0")
                p.emit("v", lambda v: v.memset(Zb[...], 0.0), writes=rZb,
                       tag="a3_z0b")
                p.emit("v", lambda v: v.memset(csc[...], 0.0), writes=[rcsc],
                       tag="a3_c0")
                for oc in range(NOUT):
                    sl = oc % 2
                    t0 = bi * S + oc * CHT
                    p.emit("y", lambda y, sl=sl, t0=t0: y.dma_start(
                        out=xnT[:, sl, :, :], in_=xnTr[:, :, t0:t0 + CHT]),
                        writes=[rxn[sl]], tag="a3_xld")
                    if oc == 0:
                        p.emit("v", lambda v: v.memset(xx[:, :, 0:1], 0.0),
                               writes=[rxx], tag="a3_xx0")
                    else:
                        p.emit("v", lambda v, sl=sl: v.tensor_copy(
                            xx[:, :, 0:1], xnT[:, 1 - sl, :, CHT - 1:CHT]),
                            reads=[rxn[1 - sl]], writes=[rxx], tag="a3_xxl")
                    if CUT < 2:
                        continue
                    p.emit("v", lambda v, sl=sl: v.tensor_copy(
                        xx[:, :, 1:CHT], xnT[:, sl, :, 0:CHT - 1]),
                        reads=[rxn[sl]], writes=[rxx], tag="a3_xxs")
                    p.emit("v", lambda v, sl=sl: v.tensor_tensor(
                        xx[:, :, :], xx[:, :, :], xnT[:, sl, :, :],
                        ALU.subtract),
                        reads=[rxn[sl]], writes=[rxx], tag="a3_xxd")
                    p.emit("v", lambda v: v.tensor_tensor(
                        tf[:, :, :], xx[:, :, :],
                        bcKT(maas[:, 0, :].rearrange("p (k o) -> p k o", o=1),
                             CHT), ALU.mult),
                        reads=[rxx, rmaas], writes=[rtf], tag="a3_xxm")
                    p.emit("v", lambda v, sl=sl: v.tensor_tensor(
                        xxx[:, :, :], tf[:, :, :], xnT[:, sl, :, :], ALU.add),
                        reads=[rtf, rxn[sl]], writes=[rxxx], tag="a3_xxx")
                    for f in range(5):
                        fs = f % 2
                        def f_l(t, f=f, fs=fs):
                            last = None
                            for k in range(KT):
                                last = t.matmul(
                                    p_m[0:32, fs, 0:CHT],
                                    mw1[:, k, f * 32:(f + 1) * 32],
                                    xxx[:, k, :],
                                    start=(k == 0), stop=(k == KT - 1))
                            return last
                        p.emit("t", f_l, reads=[rxxx, rmw1], writes=[rp_m[fs]],
                               tag="a3_lm")
                        p.emit("s", lambda s, f=f, fs=fs: s.activation(
                            lora5[:, f, :], p_m[0:32, fs, 0:CHT], AF.Tanh),
                            reads=[rp_m[fs]], writes=[rlora], tag="a3_le")
                    def mix_stage(f):
                        ms = f % 2
                        for n in range(KT):
                            s2 = n % 2
                            p.emit("t", lambda t, f=f, n=n, s2=s2: t.matmul(
                                p_m[:, s2, 0:CHT], mw2[:, f, ts(n)],
                                lora5[:, f, :], start=True, stop=True),
                                reads=[rlora, rmw2], writes=[rp_m[s2]],
                                tag="a3_mm")
                            p.emit("s", lambda s, f=f, n=n, s2=s2: s.activation(
                                tf[:, n, :], p_m[:, s2, 0:CHT], AF.Identity,
                                bias=maas[:, 1 + f, n:n + 1]),
                                reads=[rp_m[s2], rmaas], writes=[rtf],
                                tag="a3_me")
                        p.emit("v", lambda v: v.tensor_tensor(
                            tf[:, :, :], tf[:, :, :], xx[:, :, :], ALU.mult),
                            reads=[rxx], writes=[rtf], tag="a3_mx")
                        p.emit("v", lambda v, f=f, ms=ms, sl=sl: v.tensor_tensor(
                            mixs[:, ms, :, :], tf[:, :, :], xnT[:, sl, :, :],
                            ALU.add),
                            reads=[rtf, rxn[sl]], writes=[rmix[ms]],
                            tag="a3_mf")
                        return ms

                    ms = mix_stage(0)
                    def f_d1(t, ms=ms):
                        last = None
                        for k in range(KT):
                            last = t.matmul(p_m[0:64, 0, 0:CHT], dw1[:, k, :],
                                            mixs[:, ms, k, :],
                                            start=(k == 0), stop=(k == KT - 1))
                        return last
                    p.emit("t", f_d1, reads=[rmix[ms], rdw1], writes=[rp_m[0]],
                           tag="a3_d1")
                    p.emit("s", lambda s: s.activation(
                        t64[:, :], p_m[0:64, 0, 0:CHT], AF.Tanh),
                        reads=[rp_m[0]], writes=[rt64], tag="a3_d1e")
                    for n in range(KT):
                        s2 = n % 2
                        p.emit("t", lambda t, n=n, s2=s2: t.matmul(
                            p_m[:, s2, 0:CHT], dw2[:, ts(n)], t64[:, :],
                            start=True, stop=True),
                            reads=[rt64, rdw2], writes=[rp_m[s2]], tag="a3_d2")
                        p.emit("s", lambda s, n=n, s2=s2: s.activation(
                            ew[:, n, :], p_m[:, s2, 0:CHT], AF.Exp,
                            bias=tdc[:, n:n + 1]),
                            reads=[rp_m[s2], rtdc], writes=[rew], tag="a3_d2e")
                    p.emit("v", lambda v: v.tensor_scalar_mul(
                        ew[:, :, :], ew[:, :, :], -1.0),
                        writes=[rew], tag="a3_neg")
                    p.emit("v", lambda v: v.tensor_copy(
                        cs[:, :, 0:1], csc[:, :, :]),
                        reads=[rcsc], writes=[rcs], tag="a3_cs0")
                    for n in range(KT):
                        p.emit("v", lambda v, n=n: v.tensor_tensor_scan(
                            cs[:, n, 1:CHT + 1], ew[:, n, :], zcht[:, :],
                            csc[:, n, 0:1], ALU.add, ALU.add),
                            reads=[rew, rcsc, rzz], writes=[rcs],
                            tag="a3_scan")
                    p.emit("v", lambda v: v.tensor_copy(
                        csc[:, :, :], cs[:, :, CHT:CHT + 1]),
                        reads=[rcs], writes=[rcsc], tag="a3_csl")
                    ms = mix_stage(1)
                    for n in range(KT):
                        s2 = n % 2
                        def f_rk(t, ms=ms, n=n, s2=s2):
                            last = None
                            for k in range(KT):
                                last = t.matmul(
                                    p_m[:, s2, 0:CHT], wk_[:, k, ts(n)],
                                    mixs[:, ms, k, :],
                                    start=(k == 0), stop=(k == KT - 1))
                            return last
                        p.emit("t", f_rk, reads=[rmix[ms], rwk],
                               writes=[rp_m[s2]], tag="a3_km")
                        p.emit("v", lambda v, n=n, s2=s2: v.tensor_copy(
                            kT[:, n, :], p_m[:, s2, 0:CHT]),
                            reads=[rp_m[s2]], writes=[rkT], tag="a3_ke")
                    ms = mix_stage(2)
                    for tt2 in range(CHT // 128):
                        for half in range(2):
                            hs2 = half
                            def f_v(t, ms=ms, tt2=tt2, half=half, hs2=hs2):
                                last = None
                                for k in range(KT):
                                    last = t.matmul(
                                        p_m[:, hs2, :],
                                        mixs[:, ms, k,
                                             tt2 * 128:(tt2 + 1) * 128],
                                        wv_[:, k, ts(half, 512)],
                                        start=(k == 0), stop=(k == KT - 1))
                                return last
                            p.emit("t", f_v, reads=[rmix[ms], rwv],
                                   writes=[rp_m[hs2]], tag="a3_vm")
                            p.emit("v", lambda v, tt2=tt2, half=half, hs2=hs2:
                                   v.tensor_copy(
                                       vB[:, tt2, ts(half, 512)],
                                       p_m[:, hs2, :]),
                                   reads=[rp_m[hs2]], writes=[rvB],
                                   tag="a3_ve")
                    ms = mix_stage(3)
                    for n in range(KT):
                        s2 = n % 2
                        def f_rr(t, ms=ms, n=n, s2=s2):
                            last = None
                            for k in range(KT):
                                last = t.matmul(
                                    p_m[:, s2, 0:CHT], wr_[:, k, ts(n)],
                                    mixs[:, ms, k, :],
                                    start=(k == 0), stop=(k == KT - 1))
                            return last
                        p.emit("t", f_rr, reads=[rmix[ms], rwr],
                               writes=[rp_m[s2]], tag="a3_rm")
                        p.emit("v", lambda v, n=n, s2=s2: v.tensor_copy(
                            rT[:, n, :], p_m[:, s2, 0:CHT]),
                            reads=[rp_m[s2]], writes=[rrT], tag="a3_re")
                    ms = mix_stage(4)
                    def f_g1(t, ms=ms):
                        last = None
                        for k in range(KT):
                            last = t.matmul(p_m[0:64, 0, 0:CHT], gw1[:, k, :],
                                            mixs[:, ms, k, :],
                                            start=(k == 0), stop=(k == KT - 1))
                        return last
                    p.emit("t", f_g1, reads=[rmix[ms], rgw1], writes=[rp_m[0]],
                           tag="a3_g1")
                    p.emit("s", lambda s: s.activation(
                        t64[:, :], p_m[0:64, 0, 0:CHT], AF.Tanh),
                        reads=[rp_m[0]], writes=[rt64], tag="a3_g1e")
                    for tt2 in range(CHT // 128):
                        for half in range(2):
                            hs2 = half
                            p.emit("t", lambda t, tt2=tt2, half=half, hs2=hs2:
                                   t.matmul(
                                       p_m[:, hs2, :],
                                       t64[:, tt2 * 128:(tt2 + 1) * 128],
                                       gw2[:, ts(half, 512)],
                                       start=True, stop=True),
                                   reads=[rt64, rgw2], writes=[rp_m[hs2]],
                                   tag="a3_g2")
                            p.emit("v", lambda v, tt2=tt2, half=half, hs2=hs2:
                                   v.tensor_copy(
                                       gB[:, tt2, ts(half, 512)],
                                       p_m[:, hs2, :]),
                                   reads=[rp_m[hs2]], writes=[rgB],
                                   tag="a3_ge")
                    if CUT < 3:
                        continue

                    def prep_stage(ic):
                        ti0 = ic * C2
                        cs2 = ic % 2
                        p.emit("v", lambda v, ti0=ti0: v.tensor_copy(
                            bsc[:, :, 0:3], cs[:, :, ti0:ti0 + 65:32]),
                            reads=[rcs], writes=[rbsc], tag="a3_bs")
                        def set_base(i0, i1, tagn):
                            p.emit("v", lambda v: v.tensor_tensor(
                                bsc[:, :, 3:4], bsc[:, :, i0:i0 + 1],
                                bsc[:, :, i1:i1 + 1], ALU.add),
                                reads=[rbsc], writes=[rbsc], tag=tagn)
                            p.emit("v", lambda v: v.tensor_scalar_mul(
                                bsc[:, :, 3:4], bsc[:, :, 3:4], 0.5),
                                reads=[rbsc], writes=[rbsc], tag=tagn + "2")
                        set_base(0, 1, "a3_ma")
                        p.emit("v", lambda v: v.tensor_copy(
                            pce[:, :, 0:1], bsc[:, :, 3:4]),
                            reads=[rbsc], writes=[rpce], tag="a3_mac")
                        set_base(1, 2, "a3_mb")
                        jobs = [
                            (rta, rrta[cs2], ti0, ("p", 0), 1, 64,
                             rT[:, :, ti0:ti0 + 64], rrT, "a3_era"),
                            (kta, rkta[cs2], ti0 + 1, ("p", 0), -1, 32,
                             kT[:, :, ti0:ti0 + 32], rkT, "a3_eka"),
                            (rtb, rrtb[cs2], ti0 + 32, ("b", 3), 1, 32,
                             rT[:, :, ti0 + 32:ti0 + 64], rrT, "a3_erb"),
                            (ktb, rktb[cs2], ti0 + 33, ("b", 3), -1, 32,
                             kT[:, :, ti0 + 32:ti0 + 64], rkT, "a3_ekb"),
                            (rhat, rrhat[cs2], ti0, ("b", 0), 1, 64,
                             rT[:, :, ti0:ti0 + 64], rrT, "a3_erh"),
                            (k2t, rk2[cs2], ti0 + 1, ("b", 2), -1, 64,
                             kT[:, :, ti0:ti0 + 64], rkT, "a3_ek2"),
                        ]
                        def stage12(j):
                            dst, rdst, so, bid, sign, ln, msrc, mreg, tg = jobs[j]
                            es2 = j % 2
                            bias_ap = (pce[:, :, 0:1] if bid[0] == "p"
                                       else bsc[:, :, bid[1]:bid[1] + 1])
                            breg = rpce if bid[0] == "p" else rbsc
                            if sign > 0:
                                p.emit("v", lambda v, es2=es2, so=so, ln=ln,
                                       bias_ap=bias_ap: v.tensor_tensor(
                                           ex[:, es2, :, 0:ln],
                                           cs[:, :, so:so + ln],
                                           bcKT(bias_ap, ln), ALU.subtract),
                                       reads=[rcs, breg], writes=[rex[es2]],
                                       tag=tg + "t")
                            else:
                                p.emit("v", lambda v, es2=es2, so=so, ln=ln,
                                       bias_ap=bias_ap: v.tensor_tensor(
                                           ex[:, es2, :, 0:ln],
                                           bcKT(bias_ap, ln),
                                           cs[:, :, so:so + ln],
                                           ALU.subtract),
                                       reads=[rcs, breg], writes=[rex[es2]],
                                       tag=tg + "t")
                            p.emit("s", lambda s, es2=es2, ln=ln: s.activation(
                                ex[:, es2, :, 0:ln], ex[:, es2, :, 0:ln],
                                AF.Exp),
                                reads=[rex[es2]], writes=[rex[es2]],
                                tag=tg + "e")
                        def stage3(j):
                            dst, rdst, so, bid, sign, ln, msrc, mreg, tg = jobs[j]
                            es2 = j % 2
                            p.emit("v", lambda v, es2=es2, ln=ln, msrc=msrc,
                                   dst=dst, cs2=cs2: v.tensor_tensor(
                                       dst[:, cs2, :, 0:ln], msrc,
                                       ex[:, es2, :, 0:ln], ALU.mult),
                                   reads=[rex[es2], mreg], writes=[rdst],
                                   tag=tg + "m")
                        stage12(0)
                        for j in range(1, 6):
                            stage12(j)
                            stage3(j - 1)
                        stage3(5)
                        p.emit("v", lambda v: v.tensor_tensor(
                            pce[:, :, 0:1], bsc[:, :, 2:3], bsc[:, :, 0:1],
                            ALU.subtract),
                            reads=[rbsc], writes=[rpce], tag="a3_pc")
                        p.emit("s", lambda s: s.activation(
                            pce[:, :, 0:1], pce[:, :, 0:1], AF.Exp),
                            reads=[rpce], writes=[rpce], tag="a3_pce")
                        p.emit("v", lambda v, cs2=cs2: v.tensor_copy(
                            pc2[:, cs2, 0:16:2], pce[0:64, :, 0]),
                            reads=[rpce], writes=[rpc2[cs2]], tag="a3_pc2a")
                        p.emit("v", lambda v, cs2=cs2: v.tensor_copy(
                            pc2[:, cs2, 1:16:2], pce[64:128, :, 0]),
                            reads=[rpce], writes=[rpc2[cs2]], tag="a3_pc2b")
                        p.emit("v", lambda v, cs2=cs2: v.tensor_copy(
                            rh2[:, cs2, 0:16:2, :], rhat[0:64, cs2, :, :]),
                            reads=[rrhat[cs2]], writes=[rrh2[cs2]],
                            tag="a3_rh2a")
                        p.emit("v", lambda v, cs2=cs2: v.tensor_copy(
                            rh2[:, cs2, 1:16:2, :], rhat[64:128, cs2, :, :]),
                            reads=[rrhat[cs2]], writes=[rrh2[cs2]],
                            tag="a3_rh2b")
                        p.emit("v", lambda v, ti0=ti0, cs2=cs2: v.tensor_tensor(
                            e2t[:, cs2, :, :], rT[:, :, ti0:ti0 + 64],
                            kT[:, :, ti0:ti0 + 64], ALU.mult),
                            reads=[rrT, rkT], writes=[re2[cs2]], tag="a3_e2a")
                        p.emit("v", lambda v, cs2=cs2: v.tensor_tensor(
                            e2t[:, cs2, :, :], e2t[:, cs2, :, :],
                            bcKT(u_c[:, :].rearrange("p (k o) -> p k o", o=1),
                                 64), ALU.mult),
                            reads=[ru], writes=[re2[cs2]], tag="a3_e2b")
                        vbase = ti0 % 128
                        vtile = ti0 // 128
                        if vbase:
                            p.emit("v", lambda v, vbase=vbase, vtile=vtile,
                                   cs2=cs2: v.tensor_copy(
                                       vst[:, cs2, :],
                                       vB[vbase:vbase + 64, vtile, :]),
                                   reads=[rvB], writes=[rvst[cs2]],
                                   tag="a3_vst")

                    def vsrc(ic, h):
                        if ic % 2 == 0:
                            vt = (ic * C2) // 128
                            return (vB[0:64, vt, h * 64:(h + 1) * 64], rvB)
                        return (vst[:, ic % 2, h * 64:(h + 1) * 64],
                                rvst[ic % 2])

                    def a_stage(ic, h):
                        cs2 = ic % 2
                        pb = (h % 2) * 64
                        kt2 = h // 2
                        hs = h % 4
                        p.emit("t", lambda t, pb=pb, kt2=kt2, hs=hs, cs2=cs2:
                               t.matmul(
                                   p_a[0:32, hs, 0:64],
                                   kta[pb:pb + 64, cs2, kt2, 0:32],
                                   rta[pb:pb + 64, cs2, kt2, :],
                                   start=True, stop=True),
                               reads=[rkta[cs2], rrta[cs2]], writes=[rp_a[hs]],
                               tag="a3_a1")
                        p.emit("t", lambda t, pb=pb, kt2=kt2, hs=hs, cs2=cs2:
                               t.matmul(
                                   p_a[32:64, hs, 32:64],
                                   ktb[pb:pb + 64, cs2, kt2, 0:32],
                                   rtb[pb:pb + 64, cs2, kt2, 0:32],
                                   start=True, stop=True,
                                   tile_position=(pb, 32)),
                               reads=[rktb[cs2], rrtb[cs2]], writes=[rp_a[hs]],
                               tag="a3_a2")
                        p.emit("t", lambda t, pb=pb, kt2=kt2, h=h, hs=hs,
                               cs2=cs2: t.matmul(
                                   p_a[0:64, hs, 128 + 2 * (h // 2):
                                       129 + 2 * (h // 2)],
                                   e2t[pb:pb + 64, cs2, kt2, :],
                                   onesc[pb:pb + 64, :],
                                   start=True, stop=True),
                               reads=[re2[cs2], ronesc], writes=[rp_a[hs]],
                               tag="a3_dm")
                        p.emit("v", lambda v, hs=hs: v.tensor_mul(
                            at_[:, hs, :], p_a[0:64, hs, 0:64], m12c[:, :]),
                            reads=[rp_a[hs], rm12], writes=[rat_[hs]],
                            tag="a3_am")

                    def y_stage(ic, h):
                        cs2 = ic % 2
                        pr = ic // 2
                        vbase = cs2 * 64
                        hs = h % 4
                        vap, vreg = vsrc(ic, h)
                        def f_y(t, h=h, hs=hs, cs2=cs2, vap=vap):
                            t.matmul(p_a[0:64, hs, 256:320], at_[:, hs, :],
                                     vap, start=True, stop=False)
                            return t.matmul(
                                p_a[0:64, hs, 256:320], rh2[:, cs2, h, :],
                                Zb[:, h, :], start=False, stop=True)
                        p.emit("t", f_y,
                               reads=[rat_[hs], rrh2[cs2], vreg, rZb[h]],
                               writes=[rp_a[hs]], tag="a3_y")
                        p.emit("s", lambda s, h=h, hs=hs, pr=pr,
                               vbase=vbase: s.copy(
                                   Yc[vbase:vbase + 64, pr,
                                      h * 64:(h + 1) * 64],
                                   p_a[0:64, hs, 256:320]),
                               reads=[rp_a[hs]], writes=[rY[pr]], tag="a3_ye")

                    def k_stage(ic, h):
                        cs2 = ic % 2
                        pb = (h % 2) * 64
                        kt2 = h // 2
                        hs = h % 2
                        p.emit("t", lambda t, pb=pb, kt2=kt2, hs=hs, cs2=cs2:
                               t.transpose(
                                   p_tp[0:64, hs * 64:hs * 64 + 64],
                                   k2t[pb:pb + 64, cs2, kt2, :],
                                   idb[pb:pb + 64, pb:pb + 64]),
                               reads=[rk2[cs2], ridb], writes=[rp_tp],
                               tag="a3_ktp")
                        if h % 2 == 1:
                            s4 = (h - 1) % 4
                            p.emit("v", lambda v, s4=s4: v.tensor_copy(
                                k2T[:, s4:s4 + 2, :],
                                p_tp[0:64, 0:128].rearrange(
                                    "p (a b) -> p a b", a=2)),
                                reads=[rp_tp],
                                writes=[rk2T[s4], rk2T[s4 + 1]],
                                tag="a3_kte")

                    def z_stage(ic, h):
                        cs2 = ic % 2
                        hs = h % 4
                        vap, vreg = vsrc(ic, h)
                        p.emit("t", lambda t, h=h, hs=hs, vap=vap: t.matmul(
                            p_z[0:64, 0:64], k2T[:, hs, :],
                            vap, start=True, stop=True),
                            reads=[rk2T[hs], vreg], writes=[rp_z],
                            tag="a3_zm")
                        p.emit("v", lambda v, h=h, cs2=cs2:
                               v.scalar_tensor_tensor(
                                   Zf[:, h, :], Zf[:, h, :],
                                   pc2[:, cs2, h:h + 1],
                                   p_z[0:64, 0:64], ALU.mult, ALU.add),
                               reads=[rpc2[cs2], rp_z, rZf[h]],
                               writes=[rZf[h]], tag="a3_zu")
                        p.emit("v", lambda v, h=h: v.tensor_copy(
                            Zb[:, h, :], Zf[:, h, :]),
                            reads=[rZf[h]], writes=[rZb[h]], tag="a3_zb")

                    def gn_pair(pr):
                        yv = lambda t_: t_.rearrange("p (h d) -> p h d", h=16)
                        p.emit("g", lambda v, pr=pr: v.tensor_tensor(
                            yv(Yd[:, :]), yv(vB[:, pr, :]),
                            dsb[:, pr, :].rearrange("p (h o) -> p h o", o=1
                                                    ).broadcast_to(
                                [128, 16, 64]), ALU.mult),
                            reads=[rvB, rd[pr]], writes=[rYd],
                            tag="a3_yd", pool=True)
                        p.emit("g", lambda v, pr=pr: v.tensor_add(
                            Yc[:, pr, :], Yc[:, pr, :], Yd[:, :]),
                            reads=[rYd], writes=[rY[pr]], tag="a3_yda",
                            pool=True)
                        p.emit("v", lambda v, pr=pr: v.reduce_sum(
                            gst[:, 0, :], yv(Yc[:, pr, :]), axis=AX.X),
                            reads=[rY[pr]], writes=[rgst], tag="a3_gs")
                        p.emit("g", lambda v, pr=pr: v.tensor_mul(
                            Yd[:, :], Yc[:, pr, :], Yc[:, pr, :]),
                            reads=[rY[pr]], writes=[rYd], tag="a3_gq",
                            pool=True)
                        p.emit("v", lambda v: v.reduce_sum(
                            gst[:, 1, :], yv(Yd[:, :]), axis=AX.X),
                            reads=[rYd], writes=[rgst], tag="a3_gq2")
                        p.emit("v", lambda v: v.tensor_scalar_mul(
                            gst[:, 2, :], gst[:, 0, :], -1.0 / 64.0),
                            reads=[rgst], writes=[rgst], tag="a3_gm")
                        p.emit("g", lambda v: v.tensor_mul(
                            gst[:, 3, :], gst[:, 2, :], gst[:, 2, :]),
                            reads=[rgst], writes=[rgst], tag="a3_g2m",
                            pool=True)
                        p.emit("v", lambda v: v.scalar_tensor_tensor(
                            gst[:, 4, :], gst[:, 1, :], 1.0 / 64.0,
                            gst[:, 3, :], ALU.mult, ALU.subtract),
                            reads=[rgst], writes=[rgst], tag="a3_gv")
                        p.emit("s", lambda s: s.activation(
                            gst[:, 5, :], gst[:, 4, :], AF.Ln,
                            bias=eps3[:, :]),
                            reads=[rgst, reps3], writes=[rgst], tag="a3_gln")
                        p.emit("s", lambda s: s.activation(
                            gst[:, 0, :], gst[:, 5, :], AF.Exp, scale=-0.5),
                            reads=[rgst], writes=[rgst], tag="a3_gr")
                        p.emit("g", lambda v, pr=pr: v.tensor_tensor(
                            yv(Yc[:, pr, :]), yv(Yc[:, pr, :]),
                            gst[:, 2, :].rearrange("p (h o) -> p h o", o=1
                                                   ).broadcast_to(
                                [128, 16, 64]), ALU.add),
                            reads=[rgst], writes=[rY[pr]], tag="a3_gna",
                            pool=True)
                        p.emit("g", lambda v, pr=pr: v.tensor_tensor(
                            yv(Yc[:, pr, :]), yv(Yc[:, pr, :]),
                            gst[:, 0, :].rearrange("p (h o) -> p h o", o=1
                                                   ).broadcast_to(
                                [128, 16, 64]), ALU.mult),
                            reads=[rgst], writes=[rY[pr]], tag="a3_gnm",
                            pool=True)
                        p.emit("g", lambda v, pr=pr: v.tensor_tensor(
                            Yc[:, pr, :], Yc[:, pr, :], lng_t[:, :],
                            ALU.mult),
                            reads=[rlng], writes=[rY[pr]], tag="a3_glg",
                            pool=True)
                        p.emit("g", lambda v, pr=pr: v.tensor_tensor(
                            Yc[:, pr, :], Yc[:, pr, :], lnb_t[:, :],
                            ALU.add),
                            reads=[rlnb], writes=[rY[pr]], tag="a3_glb",
                            pool=True)
                        p.emit("g", lambda v, pr=pr: v.tensor_tensor(
                            gt[:, pr, :], Yc[:, pr, :], gB[:, pr, :],
                            ALU.mult),
                            reads=[rY[pr], rgB], writes=[rgt], tag="a3_gg",
                            pool=True)

                    def wo_stage(pair):
                        gtile = (t0 + pair * 128) // 128
                        def f_gtp(t, pair=pair):
                            last = None
                            for j in range(KT):
                                last = t.transpose(
                                    p_tp[:, ts(j)], gt[:, pair, ts(j)],
                                    idb[:, :])
                            return last
                        p.emit("t", f_gtp, reads=[rgt, ridb], writes=[rp_tp],
                               tag="a3_gtp")
                        p.emit("v", lambda v: v.tensor_copy(
                            gtT[:, :, :],
                            p_tp[:, :].rearrange("p (a b) -> p a b", a=KT)),
                            reads=[rp_tp], writes=[rgtT], tag="a3_gte")
                        for half in range(2):
                            hs2 = half
                            def f_om(t, half=half, hs2=hs2):
                                last = None
                                for k in range(KT):
                                    last = t.matmul(
                                        p_m[:, hs2, :], gtT[:, k, :],
                                        wo_[:, k, ts(half, 512)],
                                        start=(k == 0), stop=(k == KT - 1))
                                return last
                            p.emit("t", f_om, reads=[rgtT, rwo],
                                   writes=[rp_m[hs2]], tag="a3_om")
                            p.emit("s", lambda s, half=half, hs2=hs2:
                                   s.copy(osb[:, 0, ts(half, 512)],
                                          p_m[:, hs2, :]),
                                   reads=[rp_m[hs2]], writes=[rosb[0]],
                                   tag="a3_oe")
                        p.emit("g", lambda g, gtile=gtile:
                               g.dma_start(out=attn_hbm[ts(gtile), :],
                                           in_=osb[:, 0, :]),
                               reads=[rosb[0]], tag="a3_ost")

                    for ic in range(NIN):
                        prep_stage(ic)
                        hlist = list(range(H16)) if CUT >= 4 else []
                        for h in hlist:
                            a_stage(ic, h)
                            k_stage(ic, h)
                            if h >= 2:
                                y_stage(ic, h - 2)
                                z_stage(ic, h - 2)
                        pri = ic // 2
                        vbi = (ic % 2) * 64
                        for s0 in (range(4) if hlist else []):
                            c0 = 128 + 2 * (s0 // 2)
                            p.emit("s", lambda s, s0=s0, c0=c0, pri=pri,
                                   vbi=vbi:
                                   s.copy(dsb[vbi:vbi + 64, pri, s0:16:4],
                                          p_a[0:64, s0, c0:c0 + 16:4]),
                                   reads=[rp_a[s0]], writes=[rd[pri]],
                                   tag="a3_de")
                        for h in hlist[-2:] if hlist else []:
                            y_stage(ic, h)
                            z_stage(ic, h)
                        if CUT >= 5 and ic == 2:
                            gn_pair(0)
                        if CUT >= 5 and ic == 3:
                            wo_stage(0)
                    if CUT >= 5:
                        gn_pair(1)
                        wo_stage(1)
        p.barrier(tag="l3a_end")



def phase_l3_ffn(p, xnT_hbm, x_hbm, attn_hbm, xout_hbm, inp):
    nc = p.nc
    xnTr = xnT_hbm.rearrange("(kt r) t -> r kt t", r=128)
    with contextlib.ExitStack() as st:
        ldT, rld = cload(p, st, "l3ldT", inp["L3c_ldT"], [100, 4], BF16)
        da_, rda = cload(p, st, "l3da", inp["L3_ffn_down_aux"], [100, D], BF16)
        luT, rlu = cload(p, st, "l3luT", inp["L3c_luT"], [4, 50], BF16)
        uaT, rua = cload(p, st, "l3uaT", inp["L3c_uaT"], [50, D], BF16)
        wTf, rwT = cload(p, st, "l3WT", inp["L3c_WT"], [128, KT, D])
        br_, rbr = cload(p, st, "l3b", inp["L3c_b"], [1, D], BF16)
        ones1, ro1 = cload(p, st, "l3fo1", inp["c_ones1_bf"], [1, 128], BF16)
        with contextlib.ExitStack() as es:
            t1b = sb(es, nc, "f3_t1", [4, D], BF16)
            t2b = sb(es, nc, "f3_t2", [50, D], BF16)
            wcomb = sb(es, nc, "f3_wc", [128, KT, D], BF16)
            xnT = sb(es, nc, "f3_xnT", [128, 2, KT, 128], BF16)
            xsb = sb(es, nc, "f3_x", [128, 2, D], F32)
            atb = sb(es, nc, "f3_at", [128, 2, D], F32)
            outb = sb(es, nc, "f3_o", [128, 2, D], F32)
            p_w = pst(es, nc, "f3_p_w", [128, 512], F32)
            p_o = pst(es, nc, "f3_p_o", [128, 2, 1024], F32)
            rt1, rt2, rwc = p.reg("f3_t1"), p.reg("f3_t2"), p.reg("f3_wc")
            rxn = p.regs_n("f3_xnT", 2)
            rx = p.regs_n("f3_x", 2)
            rat = p.regs_n("f3_at", 2)
            rob = p.regs_n("f3_o", 2)
            rp_w = p.preg("f3_p_w")
            rp_o = p.pregs_n("f3_p_o", 2)
            for half in range(2):
                p.emit("t", lambda t, half=half: t.matmul(
                    p_w[0:4, 0:512], ldT[:, :], da_[:, ts(half, 512)],
                    start=True, stop=True),
                    reads=[rld, rda], writes=[rp_w], tag="f3_t1m")
                p.emit("v", lambda v, half=half: v.tensor_copy(
                    t1b[:, ts(half, 512)], p_w[0:4, 0:512]),
                    reads=[rp_w], writes=[rt1], tag="f3_t1e")
            for half in range(2):
                p.emit("t", lambda t, half=half: t.matmul(
                    p_w[0:50, 0:512], luT[:, :], t1b[:, ts(half, 512)],
                    start=True, stop=True),
                    reads=[rlu, rt1], writes=[rp_w], tag="f3_t2m")
                p.emit("v", lambda v, half=half: v.tensor_copy(
                    t2b[:, ts(half, 512)], p_w[0:50, 0:512]),
                    reads=[rp_w], writes=[rt2], tag="f3_t2e")
            for it in range(KT):
                for half in range(2):
                    p.emit("t", lambda t, it=it, half=half: t.matmul(
                        p_w[:, :], t2b[:, ts(it)], uaT[:, ts(half, 512)],
                        start=True, stop=True),
                        reads=[rt2, rua], writes=[rp_w], tag="f3_wm")
                    p.emit("v", lambda v, it=it, half=half: v.tensor_add(
                        wcomb[:, it, ts(half, 512)], p_w[:, :],
                        wTf[:, it, ts(half, 512)]),
                        reads=[rp_w, rwT], writes=[rwc], tag="f3_we")
            for tt in range(NTILE):
                sl = tt % 2
                p.emit("y", lambda y, tt=tt, sl=sl: y.dma_start(
                    out=xnT[:, sl, :, :], in_=xnTr[:, :, ts(tt)]),
                    writes=[rxn[sl]], tag="f3_xld")
                def f_o(t, sl=sl):
                    last = None
                    for half in range(2):
                        last = t.matmul(p_o[:, sl, ts(half, 512)], ones1[:, :],
                                        br_[0:1, ts(half, 512)],
                                        start=True, stop=False)
                        for k in range(KT):
                            last = t.matmul(
                                p_o[:, sl, ts(half, 512)], xnT[:, sl, k, :],
                                wcomb[:, k, ts(half, 512)],
                                start=False, stop=(k == KT - 1))
                    return last
                p.emit("t", f_o, reads=[rxn[sl], rwc, rbr, ro1],
                       writes=[rp_o[sl]], tag="f3_om")
                p.emit("y", lambda y, tt=tt, sl=sl: y.dma_start(
                    out=xsb[:, sl, :], in_=x_hbm[ts(tt), :]),
                    writes=[rx[sl]], tag="f3_xl")
                p.emit("y", lambda y, tt=tt, sl=sl: y.dma_start(
                    out=atb[:, sl, :], in_=attn_hbm[ts(tt), :]),
                    writes=[rat[sl]], tag="f3_al")
                p.emit("v", lambda v, sl=sl: v.tensor_add(
                    outb[:, sl, :], xsb[:, sl, :], atb[:, sl, :]),
                    reads=[rx[sl], rat[sl]], writes=[rob[sl]], tag="f3_a1")
                p.emit("v", lambda v, sl=sl: v.tensor_add(
                    outb[:, sl, :], outb[:, sl, :], p_o[:, sl, :]),
                    reads=[rob[sl], rp_o[sl]], writes=[rob[sl]], tag="f3_a2")
                p.emit("g", lambda g, tt=tt, sl=sl: g.dma_start(
                    out=xout_hbm[ts(tt), :], in_=outb[:, sl, :]),
                    reads=[rob[sl]], tag="f3_st")
    p.barrier(tag="l3f_end")


def l3_host_relayout(inputs, core_id):
    d = {}
    f = lambda a: np.ascontiguousarray(a, dtype=np.float32)
    for n in ("L3_attn_Wr", "L3_attn_Wk", "L3_attn_Wv", "L3_attn_Wo",
              "L3_attn_maa_w1", "L3_attn_decay_w1", "L3_attn_gate_w1",
              "L3_ffn_down_aux"):
        d[n] = f(inputs[n])
    d["L3c_maa_w2"] = f(inputs["L3_attn_maa_w2"]).transpose(1, 0, 2).copy()
    d["L3c_decay_w2"] = f(inputs["L3_attn_decay_w2"])
    d["L3c_gate_w2"] = f(inputs["L3_attn_gate_w2"])
    maa = np.stack([f(inputs["L3_attn_maa_x"]), f(inputs["L3_attn_maa_w"]),
                    f(inputs["L3_attn_maa_k"]), f(inputs["L3_attn_maa_v"]),
                    f(inputs["L3_attn_maa_r"]), f(inputs["L3_attn_maa_g"])])
    d["L3c_maa"] = maa.reshape(6, KT, 128).transpose(2, 0, 1).copy()
    d["L3c_td"] = f(inputs["L3_attn_time_decay"]).reshape(KT, 128).T.copy()
    d["L3c_u"] = f(inputs["L3_attn_u"]).reshape(-1).reshape(KT, 128).T.copy()
    d["L3c_lng"] = np.broadcast_to(f(inputs["L3_attn_ln_g"])[None, :],
                                   (128, D)).copy()
    d["L3c_lnb"] = np.broadcast_to(f(inputs["L3_attn_ln_b"])[None, :],
                                   (128, D)).copy()
    d["L3c_ldT"] = f(inputs["L3_ffn_lora_down"]).T.copy()
    d["L3c_luT"] = f(inputs["L3_ffn_lora_up"]).T.copy()
    d["L3c_uaT"] = f(inputs["L3_ffn_up_aux"]).T.copy()
    d["L3c_WT"] = f(inputs["L3_ffn_W"]).T.reshape(KT, 128, D
                                                  ).transpose(1, 0, 2).copy()
    d["L3c_b"] = f(inputs["L3_ffn_b"]).reshape(1, D)
    return d


def build_layer3(consts, debug=False):
    nc = bass.Bass("TRN2", target_bir_lowering=False)
    x_in = nc.dram_tensor("x_in", [T, D], F32, kind="ExternalInput").ap()
    x_out = nc.dram_tensor("x_out", [T, D], F32, kind="ExternalOutput").ap()
    kind = "ExternalOutput" if debug else "Internal"
    xn = nc.dram_tensor("dbg_xn", [T, D], F32, kind=kind).ap()
    xnT = nc.dram_tensor("dbg_xnT", [D, T], BF16, kind=kind).ap()
    attn = nc.dram_tensor("dbg_attn", [T, D], F32, kind=kind).ap()
    shapes = {
        "L3_attn_Wr": ([D, D], F32), "L3_attn_Wk": ([D, D], F32),
        "L3_attn_Wv": ([D, D], F32), "L3_attn_Wo": ([D, D], F32),
        "L3_attn_maa_w1": ([D, 160], F32), "L3c_maa_w2": ([32, 5, D], F32),
        "L3_attn_decay_w1": ([D, 64], F32), "L3c_decay_w2": ([64, D], F32),
        "L3_attn_gate_w1": ([D, 64], F32), "L3c_gate_w2": ([64, D], F32),
        "L3c_maa": ([128, 6, KT], F32), "L3c_td": ([128, KT], F32),
        "L3c_u": ([128, KT], F32), "L3c_lng": ([128, D], F32),
        "L3c_lnb": ([128, D], F32),
        "L3c_ldT": ([100, 4], F32), "L3_ffn_down_aux": ([100, D], F32),
        "L3c_luT": ([4, 50], F32), "L3c_uaT": ([50, D], F32),
        "L3c_WT": ([128, KT, D], F32), "L3c_b": ([1, D], F32),
    }
    inp = {}
    for n, (shp, dt) in shapes.items():
        inp[n] = nc.dram_tensor(n, shp, dt, kind="ExternalInput").ap()
    for cn, arr in consts.items():
        dt = BF16 if arr.dtype == BF else (I32 if arr.dtype == np.int32 else F32)
        inp[cn] = nc.dram_tensor(cn, list(arr.shape), dt,
                                 kind="ExternalInput").ap()
    p = Prog(nc)
    with contextlib.ExitStack() as st0:
        phase_ln(p, st0, x_in, xn, xnT, inp, want_xn=False)
    phase_l3_attn(p, xnT, attn, inp)
    phase_l3_ffn(p, xnT, x_in, attn, x_out, inp)
    p.build()
    return nc, p



L2_SHAPES = {
    "L2_attn_Wq": ([D, D], F32), "L2c_bq": ([128, KT], F32),
    "L2_ffn_wqkv": ([2, D, 1536], F32), "L2_ffn_wo": ([2, 512, D], F32),
    "L2c_qs": ([2, 128, 512], F32), "L2c_ks": ([2, 128, 512], F32),
    "L2c_w1a": ([2, D, 2816], F32), "L2c_w1g": ([2, D, 2816], F32),
    "L2c_b1": ([2, 128, 2, 22], F32), "L2c_w2": ([2, 2816, D], F32),
    "L2c_b2": ([2, 1, D], F32), "L2c_lng": ([2, 128, D], F32),
    "L2c_lnb": ([2, 128, D], F32),
}
L1_SHAPES = {
    "L1_attn_in_w": ([D, 2 * D], F32), "L1_attn_param_w": ([D, 3 * D], F32),
    "L1_attn_out_w": ([D, D], F32), "L1_ffn_down_proj": ([8, D, D], F32),
    "L1c_norm_w": ([128, KT], F32), "L1c_conv_w": ([128, KT, 4], F32),
    "L1c_conv_b": ([128, KT], F32), "L1c_skip": ([128, KT], F32),
    "L1c_in_b": ([128, 16], F32), "L1c_param_b": ([128, 24], F32),
    "L1_attn_out_b": ([1, D], F32),
    "L1_ffn_router_indices": ([128, 16, 2], I32),
    "L1_ffn_routing_weights": ([128, 16, 8], F32),
    "L1f_gu": ([8, D, 2 * D], F32),
    "L1f_gu_bias": ([1, 8, 2 * D], F32),
    "L1f_dn_bias": ([1, 8, D], F32),
}
L3_SHAPES = {
    "L3_attn_Wr": ([D, D], F32), "L3_attn_Wk": ([D, D], F32),
    "L3_attn_Wv": ([D, D], F32), "L3_attn_Wo": ([D, D], F32),
    "L3_attn_maa_w1": ([D, 160], F32), "L3c_maa_w2": ([32, 5, D], F32),
    "L3_attn_decay_w1": ([D, 64], F32), "L3c_decay_w2": ([64, D], F32),
    "L3_attn_gate_w1": ([D, 64], F32), "L3c_gate_w2": ([64, D], F32),
    "L3c_maa": ([128, 6, KT], F32), "L3c_td": ([128, KT], F32),
    "L3c_u": ([128, KT], F32), "L3c_lng": ([128, D], F32),
    "L3c_lnb": ([128, D], F32),
    "L3c_ldT": ([100, 4], F32), "L3_ffn_down_aux": ([100, D], F32),
    "L3c_luT": ([4, 50], F32), "L3c_uaT": ([50, D], F32),
    "L3c_WT": ([128, KT, D], F32), "L3c_b": ([1, D], F32),
}
L0_SHAPES = {
    "L0_attn_Wq": [D, D], "L0_attn_Wk": [D, D], "L0_attn_Wv": [D, D],
    "L0_attn_W_G": [D, D], "L0_attn_W_O": [D, D],
    "L0_ffn_Wr": [D, 16], "L0_ffn_Wg": [16, D, 512],
    "L0_ffn_Wu": [16, D, 512], "L0_ffn_Wd": [16, 512, D],
    "L0_ffn_Sg": [D, D], "L0_ffn_Su": [D, D], "L0_ffn_Sd": [D, D],
}


def build_full(consts):
    global _NAMECNT
    _NAMECNT = {}
    nc = bass.Bass("TRN2", target_bir_lowering=False)
    x_in = nc.dram_tensor("x_in", [T, D], F32, kind="ExternalInput").ap()
    x_out = nc.dram_tensor("x_out", [T, D], F32, kind="ExternalOutput").ap()
    xa = nc.dram_tensor("scr_xa", [T, D], F32, kind="Internal").ap()
    xb = nc.dram_tensor("scr_xb", [T, D], F32, kind="Internal").ap()
    xn = nc.dram_tensor("scr_xn", [T, D], F32, kind="Internal").ap()
    xnT = nc.dram_tensor("scr_xnT", [D, T], BF16, kind="Internal").ap()
    attn = nc.dram_tensor("scr_attn", [T, D], F32, kind="Internal").ap()
    ra = nc.dram_tensor("l2_ra", [T, D], F32, kind="Internal").ap()
    rb = nc.dram_tensor("l2_rb", [T, D], F32, kind="Internal").ap()
    inp = {}
    for n, shp in L0_SHAPES.items():
        inp[n] = nc.dram_tensor(n, shp, F32, kind="ExternalInput").ap()
    for shapes in (L1_SHAPES, L2_SHAPES, L3_SHAPES):
        for n, (shp, dt) in shapes.items():
            inp[n] = nc.dram_tensor(n, shp, dt, kind="ExternalInput").ap()
    for cn, arr in consts.items():
        dt = BF16 if arr.dtype == BF else (I32 if arr.dtype == np.int32 else F32)
        inp[cn] = nc.dram_tensor(cn, list(arr.shape), dt,
                                 kind="ExternalInput").ap()
    p = Prog(nc)
    with contextlib.ExitStack() as st0:
        phase_ln(p, st0, x_in, xn, xnT, inp)
    phase_l0_attn(p, xnT, attn, inp)
    phase_l0_ffn(p, xnT, x_in, attn, xa, inp)
    with contextlib.ExitStack() as st1:
        phase_ln(p, st1, xa, xn, xnT, inp)
    phase_l1_attn(p, xnT, xn, attn, inp)
    phase_l1_ffn(p, xnT, xa, attn, xb, inp)
    with contextlib.ExitStack() as st2:
        phase_ln(p, st2, xb, xn, xnT, inp)
    phase_l2_attn(p, xnT, attn, inp)
    phase_l2_ffn_attn(p, xn, ra, inp, 0)
    phase_l2_ffn_mlp(p, ra, rb, inp, 0)
    phase_l2_ffn_attn(p, rb, ra, inp, 1)
    phase_l2_ffn_mlp(p, ra, xa, inp, 1, ep=(xb, attn))
    with contextlib.ExitStack() as st3:
        phase_ln(p, st3, xa, xn, xnT, inp, want_xn=False)
    phase_l3_attn(p, xnT, attn, inp)
    phase_l3_ffn(p, xnT, xa, attn, x_out, inp)
    p.build()
    return nc, p


def kernel_fused(**inputs):
    consts = build_consts()
    nc, p = build_full(consts)
    X = np.ascontiguousarray(inputs["L0_attn_X"], dtype=np.float32)
    in_maps = []
    for c in range(NCORES):
        m = {"x_in": np.ascontiguousarray(X[2 * c:2 * c + 2].reshape(T, D))}
        for layer in range(4):
            m.update(layer_inputs(layer, inputs, consts, c))
        in_maps.append(m)
    res = run_bass_kernel_spmd(nc, in_maps, core_ids=list(range(NCORES)))
    out = np.concatenate(
        [np.asarray(r["x_out"]).reshape(B_PER_CORE, S, D)
         for r in res.results], axis=0)
    return out.astype(np.float32)


kernel = kernel_fused
